# version 02
# baseline (speedup 1.0000x reference)
import contextlib
import numpy as np
import concourse.bass as bass
import concourse.mybir as mybir
from concourse.bass_utils import run_bass_kernel_spmd

F32 = mybir.dt.float32
BF16 = mybir.dt.bfloat16
AF = mybir.ActivationFunctionType
ALU = mybir.AluOpType
AX = mybir.AxisListType

P = 128
D = 1024
I = 2816
KC = D // P
IC = I // P
TS = 2
MT = TS * P
NWARM = 4
NMAC = 16
T_CORE = NMAC * MT
T_WARM = NWARM * MT
CLAMP = float(np.float32(np.float32(6.0) * np.float32(1e-12)))
LW = 1408

ENGINES = ("pe", "dve", "act", "sp", "gp")


class Sched:

    def __init__(self, nc, es):
        self.nc = nc
        self.q = {e: [] for e in ENGINES}
        self.cnt = {e: 0 for e in ENGINES}
        self.sem = {e: es.enter_context(nc.semaphore("S_" + e)) for e in ENGINES}
        self.waited = {}

    def op(self, eng, fn, deps=(), sig=False, step=None):
        for tok in deps:
            if tok is None:
                continue
            peng, c = tok
            key = (eng, peng)
            if self.waited.get(key, 0) >= c:
                continue
            self.waited[key] = c
            s = self.sem[peng]
            self.q[eng].append(lambda E, s=s, c=c: E.wait_ge(s, c))
        if sig:
            if step is None:
                step = 16 if eng in ("sp", "gp") else 1
            self.cnt[eng] += step
            s = self.sem[eng]

            def run(E, fn=fn, s=s, step=step):
                fn(E).then_inc(s, step)
            self.q[eng].append(run)
            return (eng, self.cnt[eng])
        self.q[eng].append(lambda E, fn=fn: fn(E))
        return None


def build_program():
    nc = bass.Bass("TRN2", target_bir_lowering=False)
    es = contextlib.ExitStack()
    S = Sched(nc, es)

    u_in = nc.dram_tensor("u", [T_WARM + T_CORE, D], F32, kind="ExternalInput").ap()
    out_d = nc.dram_tensor("out", [T_CORE, D], F32, kind="ExternalOutput").ap()
    rWB = nc.dram_tensor("wbt", [D, D], F32, kind="ExternalInput").ap()
    rWC = nc.dram_tensor("wct", [D, D], F32, kind="ExternalInput").ap()
    rWD = nc.dram_tensor("wdt", [D, D], F32, kind="ExternalInput").ap()
    rWG = nc.dram_tensor("wg", [I, D], F32, kind="ExternalInput").ap()
    rWU = nc.dram_tensor("wu", [I, D], F32, kind="ExternalInput").ap()
    rWN = nc.dram_tensor("wdn", [D, I], F32, kind="ExternalInput").ap()
    logA_in = nc.dram_tensor("logA", [P, KC], F32, kind="ExternalInput").ap()
    ldt_in = nc.dram_tensor("ldt", [1, 1], F32, kind="ExternalInput").ap()
    wsc_in = nc.dram_tensor("wsc", [P, 1], F32, kind="ExternalInput").ap()
    bB_in = nc.dram_tensor("bB", [1, D], F32, kind="ExternalInput").ap()
    bC_in = nc.dram_tensor("bC", [1, D], F32, kind="ExternalInput").ap()
    bD_in = nc.dram_tensor("bD", [1, D], F32, kind="ExternalInput").ap()
    ident_in = nc.dram_tensor("ident", [P, P], F32, kind="ExternalInput").ap()
    pW = {}
    for nm, shp in (("B", [D, D]), ("C", [D, D]), ("D", [D, D]),
                    ("G", [D, I]), ("U", [D, I]), ("N", [I, D])):
        pW[nm + "h"] = nc.dram_tensor("pW%s_h" % nm, shp, BF16).ap()
        pW[nm + "l"] = nc.dram_tensor("pW%s_l" % nm, shp, BF16).ap()

    A = lambda n, s, dt=F32: nc.alloc_sbuf_tensor(n, s, dt).ap()
    ident = A("identt", [P, P])
    logA = A("logAt", [P, KC])
    ldt = A("ldtt", [1, 1])
    wsc = A("wsct", [P, 1])
    dtv = A("dtv", [1, 1])
    dt_bc = A("dt_bc", [P, 1])
    abar = A("abar", [P, KC])
    atmp = A("atmp", [P, KC])
    onesf = A("onesf", [1, P])
    ones_bf = A("ones_bf", [1, MT], BF16)
    bBh = A("bBh", [1, D], BF16)
    bBl = A("bBl", [1, D], BF16)
    bCh = A("bCh", [1, D], BF16)
    bCl = A("bCl", [1, D], BF16)
    bDh = A("bDh", [1, D], BF16)
    bDl = A("bDl", [1, D], BF16)
    state = A("state", [P, KC])
    lad1 = A("lad1", [P, LW])
    lad2 = A("lad2", [P, LW])
    lad3 = A("lad3", [P, LW])
    scm = A("scm", [P, LW // 16])
    scr = A("scr", [P, LW // 16])
    scs = A("scs", [P, LW // 16])
    lnsum = A("lnsum", [P, TS])
    lnsq = A("lnsq", [P, D])
    lnm = A("lnm", [P, TS])
    lnv = A("lnv", [P, TS])
    lnr = A("lnr", [P, TS])

    pb = [nc.alloc_psum_tensor("pb%d" % i, [P, 512], F32).ap() for i in range(8)]

    pes = contextlib.ExitStack()
    bB = pes.enter_context(nc.sbuf_tensor("bBt", [1, D], F32)).ap()
    bC = pes.enter_context(nc.sbuf_tensor("bCt", [1, D], F32)).ap()
    bD = pes.enter_context(nc.sbuf_tensor("bDt", [1, D], F32)).ap()

    t_const = None
    for (dst, src) in ((ident, ident_in), (logA, logA_in), (ldt, ldt_in),
                       (wsc, wsc_in), (bB, bB_in), (bC, bC_in), (bD, bD_in)):
        t_const = S.op("gp", lambda E, d=dst, s=src: E.dma_start(out=d[:], in_=s),
                       sig=True)
    t_ones = S.op("gp", lambda E: E.memset(ones_bf[:], 1.0), sig=True, step=1)
    t_onesf = S.op("gp", lambda E: E.memset(onesf[:], 1.0), sig=True, step=1)
    t_init = S.op("gp", lambda E: E.memset(state[:], 0.0), sig=True, step=1)

    t_dtv = S.op("act", lambda E: E.activation(out=dtv[:], in_=ldt[:], func=AF.Exp),
                 deps=[t_const], sig=True)
    t_mmdt = S.op("pe", lambda E: E.matmul(pb[0][:, 0:1], onesf[:], dtv[:],
                                           start=True, stop=True),
                  deps=[t_dtv, t_onesf], sig=True)
    t_dtbc = S.op("dve", lambda E: E.tensor_copy(out=dt_bc[:], in_=pb[0][:, 0:1]),
                  deps=[t_mmdt], sig=True)
    t_e1 = S.op("act", lambda E: E.activation(out=atmp[:], in_=logA[:], func=AF.Exp),
                deps=[t_const], sig=True)
    t_e2 = S.op("dve", lambda E: E.tensor_scalar(out=atmp[:], in0=atmp[:],
                                                 scalar1=dt_bc[:, 0:1],
                                                 scalar2=-1.0, op0=ALU.mult,
                                                 op1=ALU.mult),
                deps=[t_e1, t_dtbc], sig=True)
    t_abar = S.op("act", lambda E: E.activation(out=abar[:], in_=atmp[:], func=AF.Exp),
                  deps=[t_e2], sig=True)
    t_bbdt = S.op("dve", lambda E: E.tensor_scalar(out=bB[:], in0=bB[:],
                                                   scalar1=dt_bc[0:1, 0:1],
                                                   scalar2=None, op0=ALU.mult),
                  deps=[t_const, t_dtbc], sig=True)
    t_bias = None
    for (src, hi, lo, dep) in ((bB, bBh, bBl, t_bbdt), (bC, bCh, bCl, t_const),
                               (bD, bDh, bDl, t_const)):
        S.op("dve", lambda E, s=src, h=hi: E.tensor_copy(out=h[:], in_=s[:]),
             deps=[dep])
        t_bias = S.op("dve", lambda E, s=src, h=hi, l=lo: E.tensor_tensor(
            out=l[:], in0=s[:], in1=h[:], op=ALU.subtract), sig=True)

    pb_free = {i: [] for i in range(8)}
    pb_free[0].append(t_dtbc)

    def bank_dep(i):
        d = pb_free[i]
        pb_free[i] = []
        return d

    def emit_quant(src_ap, dst_ap, Fw, deps):
        Nb = Fw // 16
        S.op("dve", lambda E: E.scalar_tensor_tensor(
            out=lad1[:, 0:Fw], in0=src_ap, scalar=-1.0, in1=src_ap,
            op0=ALU.mult, op1=ALU.max), deps=list(deps))
        w = Fw
        lvl_off = 0
        prev = (lad1, 0)
        t_lvl = None
        while w > Nb:
            w2 = w // 2
            pt, po = prev
            dt_, do = (scm, 0) if w2 == Nb else (lad2, lvl_off)
            deps2 = [t_lvl] if t_lvl else []
            t_lvl = S.op("dve", lambda E, pt=pt, po=po, w=w, dt_=dt_, do=do, w2=w2:
                         E.tensor_tensor(out=dt_[:, do:do + w2],
                                         in0=pt[:, po:po + w:2],
                                         in1=pt[:, po + 1:po + w:2], op=ALU.max),
                         deps=deps2, sig=True)
            prev = (dt_, do)
            lvl_off += w2
            w = w2
        t1 = S.op("dve", lambda E: E.tensor_scalar(
            out=scm[:, 0:Nb], in0=scm[:, 0:Nb], scalar1=CLAMP,
            scalar2=None, op0=ALU.max), deps=[t_lvl], sig=True)
        t2 = S.op("dve", lambda E: E.reciprocal(out=scr[:, 0:Nb], in_=scm[:, 0:Nb]),
                  deps=[t1], sig=True)
        t3 = S.op("dve", lambda E: E.tensor_scalar(
            out=scr[:, 0:Nb], in0=scr[:, 0:Nb], scalar1=6.0, scalar2=None,
            op0=ALU.mult), deps=[t2], sig=True)
        t4 = S.op("dve", lambda E: E.reciprocal(out=scs[:, 0:Nb], in_=scr[:, 0:Nb]),
                  deps=[t3], sig=True)
        r6b = scr[:, 0:Nb].unsqueeze(2).broadcast_to([P, Nb, 16])
        l1v = lad1[:, 0:Fw].rearrange("p (b k) -> p b k", k=16)
        t5 = S.op("dve", lambda E: E.tensor_tensor(out=l1v, in0=l1v, in1=r6b,
                                                   op=ALU.mult),
                  deps=[t4], sig=True)
        S.op("dve", lambda E: E.tensor_scalar(out=lad2[:, 0:Fw], in0=lad1[:, 0:Fw],
                                              scalar1=0.25, scalar2=None,
                                              op0=ALU.is_gt), deps=[t5])
        for th in (0.75, 1.25, 1.75):
            S.op("dve", lambda E, th=th: E.scalar_tensor_tensor(
                out=lad2[:, 0:Fw], in0=lad1[:, 0:Fw], scalar=th,
                in1=lad2[:, 0:Fw], op0=ALU.is_gt, op1=ALU.add))
        S.op("dve", lambda E: E.tensor_scalar(out=lad3[:, 0:Fw], in0=lad1[:, 0:Fw],
                                              scalar1=2.5, scalar2=None,
                                              op0=ALU.is_gt))
        for th in (3.5, 5.0, 5.0):
            S.op("dve", lambda E, th=th: E.scalar_tensor_tensor(
                out=lad3[:, 0:Fw], in0=lad1[:, 0:Fw], scalar=th,
                in1=lad3[:, 0:Fw], op0=ALU.is_gt, op1=ALU.add))
        t_qu = S.op("dve", lambda E: E.scalar_tensor_tensor(
            out=lad2[:, 0:Fw], in0=lad2[:, 0:Fw], scalar=0.5, in1=lad3[:, 0:Fw],
            op0=ALU.mult, op1=ALU.add), sig=True)
        t_sg = S.op("act", lambda E: E.activation(out=lad3[:, 0:Fw], in_=src_ap,
                                                  func=AF.Sign),
                    deps=[t_qu], sig=True)
        sb = scs[:, 0:Nb].unsqueeze(2).broadcast_to([P, Nb, 16])
        l3v = lad3[:, 0:Fw].rearrange("p (b k) -> p b k", k=16)
        t_ss = S.op("dve", lambda E: E.tensor_tensor(out=l3v, in0=l3v, in1=sb,
                                                     op=ALU.mult),
                    deps=[t_sg], sig=True)
        t_q = S.op("dve", lambda E: E.tensor_tensor(out=dst_ap, in0=lad2[:, 0:Fw],
                                                    in1=lad3[:, 0:Fw],
                                                    op=ALU.mult),
                   deps=[t_ss], sig=True)
        return t_q

    def emit_transpose_evac(src_tile, src_cols, dst_h, dst_l, dst_cols, nblk,
                            bank_a, bank_b, deps, evac_deps=()):
        toks = []
        prev_evac = {}
        t_pe = None
        first = True
        for w0 in range(0, nblk, 4):
            nb = min(4, nblk - w0)
            bank = bank_a if (w0 // 4) % 2 == 0 else bank_b
            d = list(deps) + (bank_dep(bank) if first or True else [])
            if bank in prev_evac:
                d.append(prev_evac[bank])
            for i in range(nb):
                blk = w0 + i
                t_pe = S.op("pe", lambda E, b=bank, i=i, blk=blk: E.transpose(
                    pb[b][:, i * 128:(i + 1) * 128],
                    src_tile[:, src_cols[blk]:src_cols[blk] + 128], ident[:]),
                    deps=d if i == 0 else (), sig=(i == nb - 1))
            dc0 = dst_cols[w0]
            th = S.op("act", lambda E, b=bank, n=nb, dc0=dc0: E.activation(
                out=dst_h[:, dc0:dc0 + n * 128], in_=pb[b][:, 0:n * 128],
                func=AF.Copy), deps=[t_pe] + list(evac_deps), sig=True)
            tl = S.op("dve", lambda E, b=bank, n=nb, dc0=dc0: E.scalar_tensor_tensor(
                out=dst_l[:, dc0:dc0 + n * 128], in0=pb[b][:, 0:n * 128],
                scalar=1.0, in1=dst_h[:, dc0:dc0 + n * 128], op0=ALU.mult,
                op1=ALU.subtract), deps=[th] + list(evac_deps), sig=True)
            prev_evac[bank] = tl
            toks.append(tl)
            first = False
        for b, tl in prev_evac.items():
            pb_free[b].append(tl)
        return t_pe, toks

    def emit_mm3(psum_ap, pairs, start, stop, sig, deps):
        ops = []
        for (lh, ll, rh, rl) in pairs:
            ops.append((lh, rh))
            ops.append((lh, rl))
            ops.append((ll, rh))
        tok = None
        for i, (l, r) in enumerate(ops):
            st = start and i == 0
            last = i == len(ops) - 1
            sp = stop and last
            sg = sig and last
            tok = S.op("pe", lambda E, l=l, r=r, st=st, sp=sp: E.matmul(
                psum_ap, l, r, start=st, stop=sp),
                deps=deps if i == 0 else (), sig=sg)
        return tok

    def hview(t, r0, nchunk, c0, w):
        return t[r0:r0 + nchunk * P, c0:c0 + w].rearrange("(k p) c -> p k c", p=P)

    with nc.sbuf_tensor("wqf", [P, 22 * D], F32) as wqf_t, \
         nc.sbuf_tensor("sth", [P, 8192], BF16) as sth_t, \
         nc.sbuf_tensor("stl", [P, 8192], BF16) as stl_t:
        wqf = wqf_t.ap()
        sth = sth_t.ap()
        stl = stl_t.ap()

        wqf_free = []
        st_free = []
        prep_done = []

        for (raw, ph, pl, scale_dt) in ((rWB, pW["Bh"], pW["Bl"], True),
                                        (rWC, pW["Ch"], pW["Cl"], False),
                                        (rWD, pW["Dh"], pW["Dl"], False)):
            t_ld = S.op("gp", lambda E, raw=raw: E.dma_start(
                out=wqf[:, 0:KC * D].rearrange("p (k c) -> p k c", c=D),
                in_=hview(raw, 0, KC, 0, D)), deps=wqf_free, sig=True)
            d = [t_ld]
            if scale_dt:
                d = [S.op("dve", lambda E: E.tensor_scalar(
                    out=wqf[:, 0:KC * D], in0=wqf[:, 0:KC * D],
                    scalar1=dt_bc[:, 0:1], scalar2=None, op0=ALU.mult),
                    deps=[t_ld, t_dtbc], sig=True)]
            t_h = S.op("dve", lambda E: E.tensor_copy(out=sth[:, 0:KC * D],
                                                      in_=wqf[:, 0:KC * D]),
                       deps=d + st_free, sig=True)
            t_l = S.op("dve", lambda E: E.tensor_tensor(
                out=stl[:, 0:KC * D], in0=wqf[:, 0:KC * D],
                in1=sth[:, 0:KC * D], op=ALU.subtract), deps=[t_h], sig=True)
            t_s1 = S.op("gp", lambda E, ph=ph: E.dma_start(
                out=hview(ph, 0, KC, 0, D),
                in_=sth[:, 0:KC * D].rearrange("p (k c) -> p k c", c=D)),
                deps=[t_h], sig=True)
            t_s2 = S.op("gp", lambda E, pl=pl: E.dma_start(
                out=hview(pl, 0, KC, 0, D),
                in_=stl[:, 0:KC * D].rearrange("p (k c) -> p k c", c=D)),
                deps=[t_l], sig=True)
            wqf_free = [t_l, t_s1, t_s2]
            st_free = [t_s1, t_s2]
            prep_done += [t_s1, t_s2]

        for (raw, ph, pl) in ((rWG, pW["Gh"], pW["Gl"]), (rWU, pW["Uh"], pW["Ul"])):
            t_lds = []
            for part in range(2):
                t_lds.append(S.op("gp", lambda E, raw=raw, part=part: E.dma_start(
                    out=wqf[:, part * 11 * D:(part * 11 + 11) * D].rearrange(
                        "p (k c) -> p k c", c=D),
                    in_=hview(raw, part * 11 * P, 11, 0, D)),
                    deps=wqf_free, sig=True))
            t_q = None
            for j in range(IC):
                t_q = emit_quant(wqf[:, j * D:(j + 1) * D],
                                 wqf[:, j * D:(j + 1) * D], D,
                                 [t_lds[0] if j < 11 else t_lds[1]])
            wqf_free = []
            for kf in range(KC):
                src_cols = [j * D + kf * P for j in range(IC)]
                dst_cols = [j * P for j in range(IC)]
                tpe, evs = emit_transpose_evac(wqf, src_cols, sth, stl, dst_cols,
                                               IC, 0, 1, deps=[t_q],
                                               evac_deps=st_free)
                t_sh = S.op("gp", lambda E, ph=ph, kf=kf: E.dma_start(
                    out=ph[kf * P:(kf + 1) * P, 0:I], in_=sth[:, 0:I]),
                    deps=evs, sig=True)
                t_sl = S.op("gp", lambda E, pl=pl, kf=kf: E.dma_start(
                    out=pl[kf * P:(kf + 1) * P, 0:I], in_=stl[:, 0:I]),
                    deps=evs, sig=True)
                st_free = [t_sh, t_sl]
                prep_done += [t_sh, t_sl]
                wqf_free = [t_sh, t_sl, tpe]

        t_ldn = S.op("gp", lambda E: E.dma_start(
            out=wqf[:, 0:KC * I].rearrange("p (k c) -> p k c", c=I),
            in_=hview(rWN, 0, KC, 0, I)), deps=wqf_free, sig=True)
        t_q = None
        for r in range(KC):
            for half in range(2):
                t_q = emit_quant(
                    wqf[:, r * I + half * LW:r * I + half * LW + LW],
                    wqf[:, r * I + half * LW:r * I + half * LW + LW], LW,
                    [t_ldn])
        for kf in range(IC):
            src_cols = [r * I + kf * P for r in range(KC)]
            dst_cols = [r * P for r in range(KC)]
            tpe, evs = emit_transpose_evac(wqf, src_cols, sth, stl, dst_cols,
                                           KC, 0, 1, deps=[t_q],
                                           evac_deps=st_free)
            t_sh = S.op("gp", lambda E, kf=kf: E.dma_start(
                out=pW["Nh"][kf * P:(kf + 1) * P, 0:D], in_=sth[:, 0:D]),
                deps=evs, sig=True)
            t_sl = S.op("gp", lambda E, kf=kf: E.dma_start(
                out=pW["Nl"][kf * P:(kf + 1) * P, 0:D], in_=stl[:, 0:D]),
                deps=evs, sig=True)
            st_free = [t_sh, t_sl]
            prep_done += [t_sh, t_sl]

    pes.close()

    u_tm = [A("u_tm%d" % i, [P, TS * D]) for i in range(2)]
    xn_tm = A("xn_tm", [P, TS * D])
    xnf_h = A("xnf_h", [P, KC * MT], BF16)
    xnf_l = A("xnf_l", [P, KC * MT], BF16)
    xqf_h = A("xqf_h", [P, KC * MT], BF16)
    xqf_l = A("xqf_l", [P, KC * MT], BF16)
    xsf = A("xsf", [P, KC * MT])
    xsf_h = A("xsf_h", [P, KC * MT], BF16)
    xsf_l = A("xsf_l", [P, KC * MT], BF16)
    h_tm = A("h_tm", [P, TS * I])
    hqf_h = A("hqf_h", [P, IC * MT], BF16)
    hqf_l = A("hqf_l", [P, IC * MT], BF16)
    g_tm = A("g_tm", [P, TS * D])
    NSLOT = 3
    ring_h = [A("rng_h%d" % i, [P, 8 * 512], BF16) for i in range(NSLOT)]
    ring_l = [A("rng_l%d" % i, [P, 8 * 512], BF16) for i in range(NSLOT)]

    ring_free = [[prep_done[-1]] for _ in range(NSLOT)]
    ring_idx = [0]

    def stream_chunk(hbm_h, hbm_l, ap_fn):
        sl = ring_idx[0] % NSLOT
        ring_idx[0] += 1
        in_h, shp = ap_fn(hbm_h)
        in_l, _ = ap_fn(hbm_l)
        n, w = shp
        outv_h = ring_h[sl][:, 0:n * w].rearrange("p (k c) -> p k c", c=w)
        outv_l = ring_l[sl][:, 0:n * w].rearrange("p (k c) -> p k c", c=w)
        th = S.op("sp", lambda E, o=outv_h, i=in_h: E.dma_start(out=o, in_=i),
                  deps=ring_free[sl], sig=True)
        tl = S.op("gp", lambda E, o=outv_l, i=in_l: E.dma_start(out=o, in_=i),
                  deps=ring_free[sl], sig=True)
        ring_free[sl] = []
        return sl, th, tl

    def slot_consumed(sl, tok):
        ring_free[sl].append(tok)

    def ap_512(oc, w=512):
        def f(t):
            return (t[:, oc * 512:oc * 512 + w].rearrange(
                "(k p) c -> p k c", p=P), (KC, w))
        return f

    DN_PIECES = ((0, 8), (8, 8), (16, 6))

    def ap_wn(k0, nk, oc):
        def f(t):
            return (t[k0 * P:(k0 + nk) * P,
                      oc * 512:(oc + 1) * 512].rearrange(
                "(k p) c -> p k c", p=P), (nk, 512))
        return f

    prep_fin = prep_done[-1]
    st_m = {
        "u": [prep_fin, prep_fin],
        "out": [],
        "scan_prev": t_init,
        "state_ok": t_init,
        "xn_free": prep_fin,
        "xnf_free": prep_fin,
        "xqf_free": prep_fin,
        "xs_free": prep_fin,
        "xsfhl_free": prep_fin,
        "h_free": prep_fin,
        "hqf_free": prep_fin,
        "g_free": prep_fin,
    }

    def macro(m, warm):
        buf = m % 2
        row0 = m * MT if warm else T_WARM + m * MT
        uin_v = u_in[row0:row0 + MT, :].rearrange("(j p) f -> p j f", p=P)
        uout_v = u_tm[buf][:, 0:TS * D].rearrange("p (j f) -> p j f", f=D)
        t_ld = S.op("gp", lambda E: E.dma_start(out=uout_v, in_=uin_v),
                    deps=[st_m["u"][buf]], sig=True)
        t_ln = []
        for j in range(TS):
            uj = u_tm[buf][:, j * D:(j + 1) * D]
            ts1 = S.op("dve", lambda E, uj=uj, j=j: E.tensor_reduce(
                out=lnsum[:, j:j + 1], in_=uj, axis=AX.X, op=ALU.add),
                deps=[t_ld], sig=True)
            ts2 = S.op("act", lambda E, uj=uj, j=j: E.activation(
                out=lnsq[:, 0:D], in_=uj, func=AF.Square,
                accum_out=lnv[:, j:j + 1]), deps=[t_ld], sig=True)
            ts3 = S.op("dve", lambda E, j=j: E.tensor_scalar(
                out=lnm[:, j:j + 1], in0=lnsum[:, j:j + 1], scalar1=1.0 / D,
                scalar2=None, op0=ALU.mult), deps=[ts1], sig=True)
            ts4 = S.op("dve", lambda E, j=j: E.tensor_scalar(
                out=lnv[:, j:j + 1], in0=lnv[:, j:j + 1], scalar1=1.0 / D,
                scalar2=None, op0=ALU.mult), deps=[ts2, ts3], sig=True)
            ts5 = S.op("dve", lambda E, j=j: E.scalar_tensor_tensor(
                out=lnv[:, j:j + 1], in0=lnm[:, j:j + 1],
                scalar=lnm[:, j:j + 1], in1=lnv[:, j:j + 1], op0=ALU.mult,
                op1=ALU.subtract), deps=[ts4], sig=True)
            ts6 = S.op("dve", lambda E, j=j: E.tensor_scalar(
                out=lnv[:, j:j + 1], in0=lnv[:, j:j + 1], scalar1=-1.0,
                scalar2=1e-5, op0=ALU.mult, op1=ALU.add), deps=[ts5], sig=True)
            ts7 = S.op("act", lambda E, j=j: E.activation(
                out=lnr[:, j:j + 1], in_=lnv[:, j:j + 1], func=AF.Sqrt),
                deps=[ts6], sig=True)
            ts8 = S.op("dve", lambda E, j=j: E.reciprocal(
                out=lnr[:, j:j + 1], in_=lnr[:, j:j + 1]), deps=[ts7], sig=True)
            t9 = S.op("dve", lambda E, uj=uj, j=j: E.tensor_scalar(
                out=xn_tm[:, j * D:(j + 1) * D], in0=uj,
                scalar1=lnm[:, j:j + 1], scalar2=lnr[:, j:j + 1],
                op0=ALU.subtract, op1=ALU.mult),
                deps=[ts8, st_m["xn_free"]], sig=True)
            t_ln.append(t9)
        sc_d = []
        dc_d = []
        for k in range(KC):
            for j in range(TS):
                sc_d.append(j * D + k * P)
                dc_d.append(k * MT + j * P)
        tpe_xn, evs_xn = emit_transpose_evac(
            xn_tm, sc_d, xnf_h, xnf_l, dc_d, KC * TS, 4, 5, deps=t_ln,
            evac_deps=[st_m["xnf_free"]])
        t_mmb = []
        for half in range(2):
            sl, th, tl = stream_chunk(pW["Bh"], pW["Bl"], ap_512(half))
            for o4 in range(4):
                oc = half * 4 + o4
                bank = oc // 2
                part = oc % 2
                ps = pb[bank][:, part * MT:(part + 1) * MT]
                pairs = []
                for k in range(KC):
                    pairs.append((
                        ring_h[sl][:, k * 512 + o4 * P:k * 512 + (o4 + 1) * P],
                        ring_l[sl][:, k * 512 + o4 * P:k * 512 + (o4 + 1) * P],
                        xnf_h[:, k * MT:(k + 1) * MT],
                        xnf_l[:, k * MT:(k + 1) * MT]))
                d = [th, tl, evs_xn[-1]] + bank_dep(bank)
                emit_mm3(ps, pairs, True, False, False, d)
                S.op("pe", lambda E, ps=ps, oc=oc: E.matmul(
                    ps, bBh[:, oc * P:(oc + 1) * P], ones_bf[:, 0:MT],
                    start=False, stop=False), deps=[t_bias, t_ones])
                tok = S.op("pe", lambda E, ps=ps, oc=oc: E.matmul(
                    ps, bBl[:, oc * P:(oc + 1) * P], ones_bf[:, 0:MT],
                    start=False, stop=True), sig=True)
                t_mmb.append(tok)
            slot_consumed(sl, t_mmb[-1])
        t_scan = []
        for k in range(KC):
            bank = k // 2
            part = k % 2
            d0 = abar[:, k:k + 1].broadcast_to([P, MT])
            t_sc = S.op("dve", lambda E, k=k, bank=bank, part=part, d0=d0:
                        E.tensor_tensor_scan(
                out=xsf[:, k * MT:(k + 1) * MT], data0=d0,
                data1=pb[bank][:, part * MT:(part + 1) * MT],
                initial=state[:, k:k + 1], op0=ALU.mult, op1=ALU.add),
                deps=[t_mmb[2 * bank + 1], t_abar, st_m["state_ok"],
                      st_m["scan_prev"], st_m["xs_free"]], sig=True)
            t_scan.append(t_sc)
        t_cp = None
        for k in range(KC):
            t_cp = S.op("dve", lambda E, k=k: E.tensor_copy(
                out=state[:, k:k + 1],
                in_=xsf[:, (k + 1) * MT - 1:(k + 1) * MT]),
                deps=[t_scan[k]], sig=True)
        st_m["state_ok"] = t_cp
        st_m["scan_prev"] = t_cp
        for k in range(KC):
            pb_free[k // 2].append(t_scan[k])
        if warm:
            st_m["u"][buf] = t_ln[-1]
            st_m["xnf_free"] = t_mmb[-1]
            return
        txh = S.op("act", lambda E: E.activation(out=xsf_h[:, :], in_=xsf[:, :],
                                                 func=AF.Copy),
                   deps=[t_cp, st_m["xsfhl_free"]], sig=True)
        txl = S.op("dve", lambda E: E.tensor_tensor(
            out=xsf_l[:, :], in0=xsf[:, :], in1=xsf_h[:, :], op=ALU.subtract),
            deps=[txh], sig=True)
        st_m["xs_free"] = txl
        t_q = []
        for j in range(TS):
            t_q.append(emit_quant(xn_tm[:, j * D:(j + 1) * D],
                                  xn_tm[:, j * D:(j + 1) * D], D,
                                  [t_ln[j], tpe_xn]))
        tpe_xq, evs_xq = emit_transpose_evac(
            xn_tm, sc_d, xqf_h, xqf_l, dc_d, KC * TS, 4, 5, deps=t_q,
            evac_deps=[st_m["xqf_free"]])
        st_m["xn_free"] = tpe_xq
        t_mul = []
        last_gu = None
        for oc in range(6):
            w = 512 if oc < 5 else 256
            slg, thg, tlg = stream_chunk(pW["Gh"], pW["Gl"], ap_512(oc, w))
            slu, thu, tlu = stream_chunk(pW["Uh"], pW["Ul"], ap_512(oc, w))
            for j in range(TS):
                bkg, bku = 4 + j, 6 + j
                pairsg, pairsu = [], []
                for k in range(KC):
                    lh = xqf_h[:, k * MT + j * P:k * MT + (j + 1) * P]
                    ll = xqf_l[:, k * MT + j * P:k * MT + (j + 1) * P]
                    pairsg.append((lh, ll, ring_h[slg][:, k * w:(k + 1) * w],
                                   ring_l[slg][:, k * w:(k + 1) * w]))
                    pairsu.append((lh, ll, ring_h[slu][:, k * w:(k + 1) * w],
                                   ring_l[slu][:, k * w:(k + 1) * w]))
                tg = emit_mm3(pb[bkg][:, 0:w], pairsg, True, True, True,
                              [thg, tlg, evs_xq[-1]] + bank_dep(bkg))
                tu = emit_mm3(pb[bku][:, 0:w], pairsu, True, True, True,
                              [thu, tlu] + bank_dep(bku))
                last_gu = tu
                hcol = j * I + oc * 512
                tsil = S.op("act", lambda E, b=bkg, hcol=hcol, w=w: E.activation(
                    out=h_tm[:, hcol:hcol + w], in_=pb[b][:, 0:w], func=AF.Silu),
                    deps=[tg, st_m["h_free"]], sig=True)
                tm = S.op("dve", lambda E, b=bku, hcol=hcol, w=w: E.tensor_tensor(
                    out=h_tm[:, hcol:hcol + w], in0=h_tm[:, hcol:hcol + w],
                    in1=pb[b][:, 0:w], op=ALU.mult), deps=[tsil, tu], sig=True)
                t_mul.append(tm)
                pb_free[bkg].append(tsil)
                pb_free[bku].append(tm)
            slot_consumed(slg, t_mul[-1])
            slot_consumed(slu, t_mul[-1])
        st_m["xqf_free"] = last_gu
        t_qh = []
        for j in range(TS):
            for half in range(2):
                c0 = j * I + half * LW
                t_qh.append(emit_quant(h_tm[:, c0:c0 + LW], h_tm[:, c0:c0 + LW],
                                       LW, t_mul))
        t_cd = {}
        for oc in range(2):
            slc, thc, tlc = stream_chunk(pW["Ch"], pW["Cl"], ap_512(oc))
            sld, thd, tld = stream_chunk(pW["Dh"], pW["Dl"], ap_512(oc))
            for j in range(TS):
                bank = 4 + j * 2 + oc
                ps = pb[bank][:, 0:512]
                pairs = []
                for k in range(KC):
                    pairs.append((
                        xsf_h[:, k * MT + j * P:k * MT + (j + 1) * P],
                        xsf_l[:, k * MT + j * P:k * MT + (j + 1) * P],
                        ring_h[slc][:, k * 512:(k + 1) * 512],
                        ring_l[slc][:, k * 512:(k + 1) * 512]))
                emit_mm3(ps, pairs, True, False, False,
                         [thc, tlc, txl] + bank_dep(bank))
                pairs = []
                for k in range(KC):
                    pairs.append((
                        xnf_h[:, k * MT + j * P:k * MT + (j + 1) * P],
                        xnf_l[:, k * MT + j * P:k * MT + (j + 1) * P],
                        ring_h[sld][:, k * 512:(k + 1) * 512],
                        ring_l[sld][:, k * 512:(k + 1) * 512]))
                emit_mm3(ps, pairs, False, False, False, [thd, tld])
                for bi, (bv, stp) in enumerate(((bCh, False), (bCl, False),
                                                (bDh, False), (bDl, True))):
                    tok = S.op("pe", lambda E, ps=ps, oc=oc, bv=bv, stp=stp:
                               E.matmul(
                        ps, ones_bf[:, 0:P], bv[:, oc * 512:(oc + 1) * 512],
                        start=False, stop=stp), deps=[t_bias] if bi == 0 else (),
                        sig=stp)
                t_cd[(j, oc)] = tok
            slot_consumed(slc, t_cd[(TS - 1, oc)])
            slot_consumed(sld, t_cd[(TS - 1, oc)])
        st_m["xnf_free"] = t_cd[(TS - 1, 1)]
        st_m["xsfhl_free"] = t_cd[(TS - 1, 1)]
        sc_h = []
        dc_h = []
        for kf in range(IC):
            for j in range(TS):
                sc_h.append(j * I + kf * P)
                dc_h.append(kf * MT + j * P)
        tpe_hq, evs_hq = emit_transpose_evac(
            h_tm, sc_h, hqf_h, hqf_l, dc_h, IC * TS, 0, 1, deps=t_qh,
            evac_deps=[st_m["hqf_free"]])
        st_m["h_free"] = tpe_hq
        t_down = {}
        tok = None
        for oc in range(2):
            for pi, (k0, nk) in enumerate(DN_PIECES):
                sl, th, tl = stream_chunk(pW["Nh"], pW["Nl"], ap_wn(k0, nk, oc))
                for j in range(TS):
                    bank = 2 + j
                    ps = pb[bank][:, 0:512]
                    pairs = []
                    for kk in range(nk):
                        kf = k0 + kk
                        pairs.append((
                            hqf_h[:, kf * MT + j * P:kf * MT + (j + 1) * P],
                            hqf_l[:, kf * MT + j * P:kf * MT + (j + 1) * P],
                            ring_h[sl][:, kk * 512:(kk + 1) * 512],
                            ring_l[sl][:, kk * 512:(kk + 1) * 512]))
                    d = [th, tl, evs_hq[-1]]
                    if pi == 0:
                        d += bank_dep(bank)
                    tok = emit_mm3(ps, pairs, pi == 0,
                                   pi == len(DN_PIECES) - 1, True, d)
                slot_consumed(sl, tok)
            for j in range(TS):
                bank = 2 + j
                gcol = j * D + oc * 512
                tg2 = S.op("act", lambda E, b=bank, gcol=gcol: E.activation(
                    out=g_tm[:, gcol:gcol + 512], in_=pb[b][:, 0:512],
                    func=AF.Silu), deps=[tok, st_m["g_free"]], sig=True)
                t_down[(j, oc)] = tg2
                pb_free[bank].append(tg2)
        st_m["hqf_free"] = tok
        t_comb = None
        for j in range(TS):
            for oc in range(2):
                bank = 4 + j * 2 + oc
                gcol = j * D + oc * 512
                t1 = S.op("dve", lambda E, b=bank, gcol=gcol: E.tensor_tensor(
                    out=g_tm[:, gcol:gcol + 512], in0=g_tm[:, gcol:gcol + 512],
                    in1=pb[b][:, 0:512], op=ALU.mult),
                    deps=[t_down[(j, oc)], t_cd[(j, oc)]], sig=True)
                t_comb = S.op("dve", lambda E, gcol=gcol, buf=buf: E.tensor_tensor(
                    out=u_tm[buf][:, gcol:gcol + 512],
                    in0=u_tm[buf][:, gcol:gcol + 512],
                    in1=g_tm[:, gcol:gcol + 512], op=ALU.add),
                    deps=[t1], sig=True)
                pb_free[bank].append(t_comb)
        st_m["g_free"] = t_comb
        orow = m * MT
        o_v = out_d[orow:orow + MT, :].rearrange("(j p) f -> p j f", p=P)
        i_v = u_tm[buf][:, 0:TS * D].rearrange("p (j f) -> p j f", f=D)
        t_o = S.op("gp", lambda E: E.dma_start(out=o_v, in_=i_v),
                   deps=[t_comb], sig=True)
        st_m["out"].append(t_o)
        st_m["u"][buf] = t_o

    for wm in range(NWARM):
        macro(wm, warm=True)
    st_m["state_ok"] = S.op("dve", lambda E: E.tensor_scalar(
        out=state[:, :], in0=state[:, :], scalar1=wsc[:, 0:1], scalar2=None,
        op0=ALU.mult), deps=[st_m["state_ok"], t_const], sig=True)
    st_m["scan_prev"] = st_m["state_ok"]
    for m in range(NMAC):
        macro(m, warm=False)

    S.op("gp", lambda E: E.wait_ge(S.sem["gp"], S.cnt["gp"]))
    S.op("sp", lambda E: E.wait_ge(S.sem["sp"], S.cnt["sp"]))

    with nc.Block() as block:
        @block.tensor
        def _(E):
            for fn in S.q["pe"]:
                fn(E)

        @block.vector
        def _(E):
            for fn in S.q["dve"]:
                fn(E)

        @block.scalar
        def _(E):
            for fn in S.q["act"]:
                fn(E)

        @block.sync
        def _(E):
            for fn in S.q["sp"]:
                fn(E)

        @block.gpsimd
        def _(E):
            for fn in S.q["gp"]:
                fn(E)

    es.close()
    return nc


def kernel(**inputs):
    u = np.asarray(inputs["L0_attn_u"], dtype=np.float32)
    log_A = np.asarray(inputs["L0_attn_log_A"], dtype=np.float32)
    W_B = np.asarray(inputs["L0_attn_W_B"], dtype=np.float32)
    b_B = np.asarray(inputs["L0_attn_b_B"], dtype=np.float32)
    W_C = np.asarray(inputs["L0_attn_W_C"], dtype=np.float32)
    b_C = np.asarray(inputs["L0_attn_b_C"], dtype=np.float32)
    W_D = np.asarray(inputs["L0_attn_W_D"], dtype=np.float32)
    b_D = np.asarray(inputs["L0_attn_b_D"], dtype=np.float32)
    log_dt = np.asarray(inputs["L0_attn_log_dt"], dtype=np.float32)
    w_gate = np.asarray(inputs["L0_ffn_w_gate"], dtype=np.float32)
    w_up = np.asarray(inputs["L0_ffn_w_up"], dtype=np.float32)
    w_down = np.asarray(inputs["L0_ffn_w_down"], dtype=np.float32)

    WBt = np.ascontiguousarray(W_B.T)
    WCt = np.ascontiguousarray(W_C.T)
    WDt = np.ascontiguousarray(W_D.T)
    logA_t = np.ascontiguousarray(log_A.reshape(KC, P).T)
    ident = np.eye(P, dtype=np.float32)

    nc = build_program()
    in_maps = []
    for c in range(8):
        b, q = c // 4, c % 4
        t0 = q * T_CORE
        wstart = t0 - T_WARM if q > 0 else 0
        u_c = np.ascontiguousarray(
            np.concatenate([u[b, wstart:wstart + T_WARM], u[b, t0:t0 + T_CORE]],
                           axis=0))
        wsc_v = np.full((P, 1), 1.0 if q > 0 else 0.0, dtype=np.float32)
        in_maps.append({
            "u": u_c,
            "wbt": WBt, "wct": WCt, "wdt": WDt,
            "wg": w_gate, "wu": w_up, "wdn": w_down,
            "logA": logA_t, "ldt": log_dt.reshape(1, 1),
            "wsc": wsc_v,
            "bB": b_B.reshape(1, D), "bC": b_C.reshape(1, D),
            "bD": b_D.reshape(1, D), "ident": ident,
        })
    res = run_bass_kernel_spmd(nc, in_maps, core_ids=list(range(8)))
    out = np.empty((2, 16384, D), dtype=np.float32)
    for c in range(8):
        b, q = c // 4, c % 4
        out[b, q * T_CORE:(q + 1) * T_CORE] = res.results[c]["out"]
    return out


# version 03
# speedup vs baseline: 1.2439x; 1.2439x over previous
import contextlib
import numpy as np
import concourse.bass as bass
import concourse.mybir as mybir
from concourse.bass_utils import run_bass_kernel_spmd

F32 = mybir.dt.float32
BF16 = mybir.dt.bfloat16
AF = mybir.ActivationFunctionType
ALU = mybir.AluOpType
AX = mybir.AxisListType

P = 128
D = 1024
I = 2816
KC = D // P
IC = I // P
TS = 2
MT = TS * P
NWARM = 4
NMAC = 16
T_CORE = NMAC * MT
T_WARM = NWARM * MT
CLAMP = float(np.float32(np.float32(6.0) * np.float32(1e-12)))
LW = 1408
DN_PIECES = ((0, 8), (8, 8), (16, 6))

ENGINES = ("pe", "dve", "act", "sp", "gp")


class Sched:
    def __init__(self, nc, es):
        self.nc = nc
        self.q = {e: [] for e in ENGINES}
        self.cnt = {e: 0 for e in ENGINES}
        self.sem = {e: es.enter_context(nc.semaphore("S_" + e)) for e in ENGINES}
        self.waited = {}

    def op(self, eng, fn, deps=(), sig=False, step=None):
        for tok in deps:
            if tok is None:
                continue
            peng, c = tok
            key = (eng, peng)
            if self.waited.get(key, 0) >= c:
                continue
            self.waited[key] = c
            s = self.sem[peng]
            self.q[eng].append(lambda E, s=s, c=c: E.wait_ge(s, c))
        if sig:
            if step is None:
                step = 16 if eng in ("sp", "gp") else 1
            self.cnt[eng] += step
            s = self.sem[eng]

            def run(E, fn=fn, s=s, step=step):
                fn(E).then_inc(s, step)
            self.q[eng].append(run)
            return (eng, self.cnt[eng])
        self.q[eng].append(lambda E, fn=fn: fn(E))
        return None


def build_program():
    nc = bass.Bass("TRN2", target_bir_lowering=False)
    es = contextlib.ExitStack()
    S = Sched(nc, es)

    u_in = nc.dram_tensor("u", [T_WARM + T_CORE, D], F32, kind="ExternalInput").ap()
    out_d = nc.dram_tensor("out", [T_CORE, D], F32, kind="ExternalOutput").ap()
    rWB = nc.dram_tensor("wbt", [D, D], F32, kind="ExternalInput").ap()
    rWC = nc.dram_tensor("wct", [D, D], F32, kind="ExternalInput").ap()
    rWD = nc.dram_tensor("wdt", [D, D], F32, kind="ExternalInput").ap()
    rWG = nc.dram_tensor("wg", [I, D], F32, kind="ExternalInput").ap()
    rWU = nc.dram_tensor("wu", [I, D], F32, kind="ExternalInput").ap()
    rWN = nc.dram_tensor("wdn", [D, I], F32, kind="ExternalInput").ap()
    logA_in = nc.dram_tensor("logA", [P, KC], F32, kind="ExternalInput").ap()
    ldt_in = nc.dram_tensor("ldt", [1, 1], F32, kind="ExternalInput").ap()
    wsc_in = nc.dram_tensor("wsc", [P, 1], F32, kind="ExternalInput").ap()
    bB_in = nc.dram_tensor("bB", [1, D], F32, kind="ExternalInput").ap()
    bC_in = nc.dram_tensor("bC", [1, D], F32, kind="ExternalInput").ap()
    bD_in = nc.dram_tensor("bD", [1, D], F32, kind="ExternalInput").ap()
    ident_in = nc.dram_tensor("ident", [P, P], F32, kind="ExternalInput").ap()
    pW = {}
    for nm, cols in (("B", 2 * 4096), ("C", 2 * 4096), ("D", 2 * 4096),
                     ("G", 6 * 4096), ("U", 6 * 4096), ("N", 2 * 22 * 512)):
        pW[nm + "h"] = nc.dram_tensor("pW%s_h" % nm, [P, cols], BF16).ap()
        pW[nm + "l"] = nc.dram_tensor("pW%s_l" % nm, [P, cols], BF16).ap()

    A = lambda n, s, dt=F32: nc.alloc_sbuf_tensor(n, s, dt).ap()
    ident = A("identt", [P, P])
    logA = A("logAt", [P, KC])
    ldt = A("ldtt", [1, 1])
    wsc = A("wsct", [P, 1])
    dtv = A("dtv", [1, 1])
    dt_bc = A("dt_bc", [P, 1])
    abar = A("abar", [P, KC])
    atmp = A("atmp", [P, KC])
    onesf = A("onesf", [1, P])
    ones_bf = A("ones_bf", [1, MT], BF16)
    bBh = A("bBh", [1, D], BF16)
    bBl = A("bBl", [1, D], BF16)
    bCh = A("bCh", [1, D], BF16)
    bCl = A("bCl", [1, D], BF16)
    bDh = A("bDh", [1, D], BF16)
    bDl = A("bDl", [1, D], BF16)
    state = A("state", [P, KC])
    lad1 = A("lad1", [P, LW])
    lad2 = A("lad2", [P, LW])
    lad3 = A("lad3", [P, LW])
    scm = A("scm", [P, LW // 16])
    scr = A("scr", [P, LW // 16])
    scs = A("scs", [P, LW // 16])
    lnsum = A("lnsum", [P, TS])
    lnsq = A("lnsq", [P, D])
    lnm = A("lnm", [P, TS])
    lnv = A("lnv", [P, TS])
    lnr = A("lnr", [P, TS])

    pb = [nc.alloc_psum_tensor("pb%d" % i, [P, 512], F32).ap() for i in range(8)]

    pes = contextlib.ExitStack()
    bB = pes.enter_context(nc.sbuf_tensor("bBt", [1, D], F32)).ap()
    bC = pes.enter_context(nc.sbuf_tensor("bCt", [1, D], F32)).ap()
    bD = pes.enter_context(nc.sbuf_tensor("bDt", [1, D], F32)).ap()

    t_const = None
    for (dst, src) in ((ident, ident_in), (logA, logA_in), (ldt, ldt_in),
                       (wsc, wsc_in), (bB, bB_in), (bC, bC_in), (bD, bD_in)):
        t_const = S.op("gp", lambda E, d=dst, s=src: E.dma_start(out=d[:], in_=s),
                       sig=True)
    t_ones = S.op("gp", lambda E: E.memset(ones_bf[:], 1.0), sig=True, step=1)
    t_onesf = S.op("gp", lambda E: E.memset(onesf[:], 1.0), sig=True, step=1)
    t_init = S.op("gp", lambda E: E.memset(state[:], 0.0), sig=True, step=1)

    t_dtv = S.op("act", lambda E: E.activation(out=dtv[:], in_=ldt[:], func=AF.Exp),
                 deps=[t_const], sig=True)
    t_mmdt = S.op("pe", lambda E: E.matmul(pb[0][:, 0:1], onesf[:], dtv[:],
                                           start=True, stop=True),
                  deps=[t_dtv, t_onesf], sig=True)
    t_dtbc = S.op("dve", lambda E: E.tensor_copy(out=dt_bc[:], in_=pb[0][:, 0:1]),
                  deps=[t_mmdt], sig=True)
    t_e1 = S.op("act", lambda E: E.activation(out=atmp[:], in_=logA[:], func=AF.Exp),
                deps=[t_const], sig=True)
    t_e2 = S.op("dve", lambda E: E.tensor_scalar(out=atmp[:], in0=atmp[:],
                                                 scalar1=dt_bc[:, 0:1],
                                                 scalar2=-1.0, op0=ALU.mult,
                                                 op1=ALU.mult),
                deps=[t_e1, t_dtbc], sig=True)
    t_abar = S.op("act", lambda E: E.activation(out=abar[:], in_=atmp[:], func=AF.Exp),
                  deps=[t_e2], sig=True)
    t_bbdt = S.op("dve", lambda E: E.tensor_scalar(out=bB[:], in0=bB[:],
                                                   scalar1=dt_bc[0:1, 0:1],
                                                   scalar2=None, op0=ALU.mult),
                  deps=[t_const, t_dtbc], sig=True)
    t_bias = None
    for (src, hi, lo, dep) in ((bB, bBh, bBl, t_bbdt), (bC, bCh, bCl, t_const),
                               (bD, bDh, bDl, t_const)):
        S.op("dve", lambda E, s=src, h=hi: E.tensor_copy(out=h[:], in_=s[:]),
             deps=[dep])
        t_bias = S.op("dve", lambda E, s=src, h=hi, l=lo: E.tensor_tensor(
            out=l[:], in0=s[:], in1=h[:], op=ALU.subtract), sig=True)

    pb_free = {i: [] for i in range(8)}
    pb_free[0].append(t_dtbc)

    def bank_dep(i):
        d = pb_free[i]
        pb_free[i] = []
        return d

    def emit_quant(src_ap, dst_ap, Fw, deps):
        Nb = Fw // 16
        t_abs = S.op("act", lambda E: E.activation(out=lad1[:, 0:Fw], in_=src_ap,
                                                   func=AF.Abs),
                     deps=list(deps), sig=True)
        w = Fw
        lvl_off = 0
        prev = (lad1, 0)
        t_lvl = t_abs
        while w > Nb:
            w2 = w // 2
            pt, po = prev
            dt_, do = (scm, 0) if w2 == Nb else (lad2, lvl_off)
            t_lvl = S.op("dve", lambda E, pt=pt, po=po, w=w, dt_=dt_, do=do, w2=w2:
                         E.tensor_tensor(out=dt_[:, do:do + w2],
                                         in0=pt[:, po:po + w:2],
                                         in1=pt[:, po + 1:po + w:2], op=ALU.max),
                         deps=[t_lvl], sig=True)
            prev = (dt_, do)
            lvl_off += w2
            w = w2
        t1 = S.op("dve", lambda E: E.tensor_scalar(
            out=scm[:, 0:Nb], in0=scm[:, 0:Nb], scalar1=CLAMP,
            scalar2=None, op0=ALU.max), deps=[t_lvl], sig=True)
        t2 = S.op("dve", lambda E: E.reciprocal(out=scr[:, 0:Nb], in_=scm[:, 0:Nb]),
                  deps=[t1], sig=True)
        t3 = S.op("dve", lambda E: E.tensor_scalar(
            out=scr[:, 0:Nb], in0=scr[:, 0:Nb], scalar1=6.0, scalar2=None,
            op0=ALU.mult), deps=[t2], sig=True)
        t4 = S.op("dve", lambda E: E.reciprocal(out=scs[:, 0:Nb], in_=scr[:, 0:Nb]),
                  deps=[t3], sig=True)
        r6b = scr[:, 0:Nb].unsqueeze(2).broadcast_to([P, Nb, 16])
        l1v = lad1[:, 0:Fw].rearrange("p (b k) -> p b k", k=16)
        t5 = S.op("dve", lambda E: E.tensor_tensor(out=l1v, in0=l1v, in1=r6b,
                                                   op=ALU.mult),
                  deps=[t4], sig=True)
        S.op("dve", lambda E: E.tensor_scalar(out=lad2[:, 0:Fw], in0=lad1[:, 0:Fw],
                                              scalar1=0.25, scalar2=None,
                                              op0=ALU.is_gt), deps=[t5])
        for th in (0.75, 1.25, 1.75):
            S.op("dve", lambda E, th=th: E.scalar_tensor_tensor(
                out=lad2[:, 0:Fw], in0=lad1[:, 0:Fw], scalar=th,
                in1=lad2[:, 0:Fw], op0=ALU.is_gt, op1=ALU.add))
        S.op("dve", lambda E: E.tensor_scalar(out=lad3[:, 0:Fw], in0=lad1[:, 0:Fw],
                                              scalar1=2.5, scalar2=None,
                                              op0=ALU.is_gt))
        for th in (3.5, 5.0, 5.0):
            S.op("dve", lambda E, th=th: E.scalar_tensor_tensor(
                out=lad3[:, 0:Fw], in0=lad1[:, 0:Fw], scalar=th,
                in1=lad3[:, 0:Fw], op0=ALU.is_gt, op1=ALU.add))
        t_qu = S.op("dve", lambda E: E.scalar_tensor_tensor(
            out=lad2[:, 0:Fw], in0=lad2[:, 0:Fw], scalar=0.5, in1=lad3[:, 0:Fw],
            op0=ALU.mult, op1=ALU.add), sig=True)
        t_sg = S.op("act", lambda E: E.activation(out=lad3[:, 0:Fw], in_=src_ap,
                                                  func=AF.Sign),
                    deps=[t_qu], sig=True)
        sb = scs[:, 0:Nb].unsqueeze(2).broadcast_to([P, Nb, 16])
        l3v = lad3[:, 0:Fw].rearrange("p (b k) -> p b k", k=16)
        t_ss = S.op("dve", lambda E: E.tensor_tensor(out=l3v, in0=l3v, in1=sb,
                                                     op=ALU.mult),
                    deps=[t_sg], sig=True)
        t_q = S.op("dve", lambda E: E.tensor_tensor(out=dst_ap, in0=lad2[:, 0:Fw],
                                                    in1=lad3[:, 0:Fw],
                                                    op=ALU.mult),
                   deps=[t_ss], sig=True)
        return t_q

    def emit_transpose_evac(src_tile, src_cols, dst_h, dst_l, dst_cols, nblk,
                            bank_a, bank_b, deps, evac_deps=()):
        toks = []
        prev_evac = {}
        t_pe = None
        for w0 in range(0, nblk, 4):
            nb = min(4, nblk - w0)
            bank = bank_a if (w0 // 4) % 2 == 0 else bank_b
            d = list(deps) + bank_dep(bank)
            if bank in prev_evac:
                d.append(prev_evac[bank])
            for i in range(nb):
                blk = w0 + i
                t_pe = S.op("pe", lambda E, b=bank, i=i, blk=blk: E.transpose(
                    pb[b][:, i * 128:(i + 1) * 128],
                    src_tile[:, src_cols[blk]:src_cols[blk] + 128], ident[:]),
                    deps=d if i == 0 else (), sig=(i == nb - 1))
            dc0 = dst_cols[w0]
            th = S.op("act", lambda E, b=bank, n=nb, dc0=dc0: E.activation(
                out=dst_h[:, dc0:dc0 + n * 128], in_=pb[b][:, 0:n * 128],
                func=AF.Copy), deps=[t_pe] + list(evac_deps), sig=True)
            tl = S.op("dve", lambda E, b=bank, n=nb, dc0=dc0: E.scalar_tensor_tensor(
                out=dst_l[:, dc0:dc0 + n * 128], in0=pb[b][:, 0:n * 128],
                scalar=1.0, in1=dst_h[:, dc0:dc0 + n * 128], op0=ALU.mult,
                op1=ALU.subtract), deps=[th] + list(evac_deps), sig=True)
            prev_evac[bank] = tl
            toks.append(tl)
        for b, tl in prev_evac.items():
            pb_free[b].append(tl)
        return t_pe, toks

    def emit_mm3(psum_ap, pairs, start, stop, sig, deps):
        ops = []
        for (lh, ll, rh, rl) in pairs:
            ops.append((lh, rh))
            ops.append((lh, rl))
            ops.append((ll, rh))
        tok = None
        for i, (l, r) in enumerate(ops):
            st = start and i == 0
            last = i == len(ops) - 1
            tok = S.op("pe", lambda E, l=l, r=r, st=st, sp=stop and last:
                       E.matmul(psum_ap, l, r, start=st, stop=sp),
                       deps=deps if i == 0 else (), sig=sig and last)
        return tok

    def hview(t, r0, nchunk, c0, w):
        return t[r0:r0 + nchunk * P, c0:c0 + w].rearrange("(k p) c -> p k c", p=P)

    with nc.sbuf_tensor("wqf", [P, 22 * D], F32) as wqf_t, \
         nc.sbuf_tensor("sth", [P, 8192], BF16) as sth_t, \
         nc.sbuf_tensor("stl", [P, 8192], BF16) as stl_t:
        wqf = wqf_t.ap()
        sth = sth_t.ap()
        stl = stl_t.ap()

        wqf_free = []
        st_free = []
        prep_done = []

        for (raw, ph, pl, scale_dt) in ((rWB, pW["Bh"], pW["Bl"], True),
                                        (rWC, pW["Ch"], pW["Cl"], False),
                                        (rWD, pW["Dh"], pW["Dl"], False)):
            t_ld = S.op("gp", lambda E, raw=raw: E.dma_start(
                out=wqf[:, 0:KC * D].rearrange("p (k c) -> p k c", c=D),
                in_=hview(raw, 0, KC, 0, D)), deps=wqf_free, sig=True)
            d = [t_ld]
            if scale_dt:
                d = [S.op("dve", lambda E: E.tensor_scalar(
                    out=wqf[:, 0:KC * D], in0=wqf[:, 0:KC * D],
                    scalar1=dt_bc[:, 0:1], scalar2=None, op0=ALU.mult),
                    deps=[t_ld, t_dtbc], sig=True)]
            t_h = S.op("dve", lambda E: E.tensor_copy(out=sth[:, 0:KC * D],
                                                      in_=wqf[:, 0:KC * D]),
                       deps=d + st_free, sig=True)
            t_l = S.op("dve", lambda E: E.tensor_tensor(
                out=stl[:, 0:KC * D], in0=wqf[:, 0:KC * D],
                in1=sth[:, 0:KC * D], op=ALU.subtract), deps=[t_h], sig=True)
            stores = []
            for oc in range(2):
                sb_v = sth[:, 0:KC * D].rearrange(
                    "p (k c) -> p k c", c=D)[:, :, oc * 512:(oc + 1) * 512]
                sb_vl = stl[:, 0:KC * D].rearrange(
                    "p (k c) -> p k c", c=D)[:, :, oc * 512:(oc + 1) * 512]
                hb = ph[:, oc * 4096:(oc + 1) * 4096].rearrange(
                    "p (k c) -> p k c", c=512)
                hbl = pl[:, oc * 4096:(oc + 1) * 4096].rearrange(
                    "p (k c) -> p k c", c=512)
                stores.append(S.op("gp", lambda E, o=hb, i=sb_v: E.dma_start(
                    out=o, in_=i), deps=[t_h], sig=True))
                stores.append(S.op("gp", lambda E, o=hbl, i=sb_vl: E.dma_start(
                    out=o, in_=i), deps=[t_l], sig=True))
            wqf_free = [t_l] + stores
            st_free = stores
            prep_done += stores

        for (raw, ph, pl) in ((rWG, pW["Gh"], pW["Gl"]), (rWU, pW["Uh"], pW["Ul"])):
            t_lds = []
            for part in range(2):
                t_lds.append(S.op("gp", lambda E, raw=raw, part=part: E.dma_start(
                    out=wqf[:, part * 11 * D:(part * 11 + 11) * D].rearrange(
                        "p (k c) -> p k c", c=D),
                    in_=hview(raw, part * 11 * P, 11, 0, D)),
                    deps=wqf_free, sig=True))
            t_q = None
            for j in range(IC):
                t_q = emit_quant(wqf[:, j * D:(j + 1) * D],
                                 wqf[:, j * D:(j + 1) * D], D,
                                 [t_lds[0] if j < 11 else t_lds[1]])
            wqf_free = []
            for kf in range(KC):
                src_cols = [j * D + kf * P for j in range(IC)]
                dst_cols = [j * P for j in range(IC)]
                tpe, evs = emit_transpose_evac(wqf, src_cols, sth, stl, dst_cols,
                                               IC, 0, 1, deps=[t_q],
                                               evac_deps=st_free)
                stores = []
                for (buf, hbm) in ((sth, ph), (stl, pl)):
                    in_a = buf[:, 0:5 * 512].rearrange("p (o c) -> p o c", c=512)
                    out_a = bass.AP(hbm.tensor, kf * 512,
                                    [[6 * 4096, P], [4096, 5], [1, 512]])
                    stores.append(S.op("gp", lambda E, o=out_a, i=in_a:
                                       E.dma_start(out=o, in_=i),
                                       deps=evs, sig=True))
                    in_b = buf[:, 5 * 512:5 * 512 + 256]
                    out_b = bass.AP(hbm.tensor, 5 * 4096 + kf * 256,
                                    [[6 * 4096, P], [1, 256]])
                    stores.append(S.op("gp", lambda E, o=out_b, i=in_b:
                                       E.dma_start(out=o, in_=i),
                                       deps=evs, sig=True))
                st_free = stores
                prep_done += stores
                wqf_free = stores + [tpe]

        t_ldn = S.op("gp", lambda E: E.dma_start(
            out=wqf[:, 0:KC * I].rearrange("p (k c) -> p k c", c=I),
            in_=hview(rWN, 0, KC, 0, I)), deps=wqf_free, sig=True)
        t_q = None
        for r in range(KC):
            for half in range(2):
                t_q = emit_quant(
                    wqf[:, r * I + half * LW:r * I + half * LW + LW],
                    wqf[:, r * I + half * LW:r * I + half * LW + LW], LW,
                    [t_ldn])
        for kf in range(IC):
            src_cols = [r * I + kf * P for r in range(KC)]
            dst_cols = [r * P for r in range(KC)]
            tpe, evs = emit_transpose_evac(wqf, src_cols, sth, stl, dst_cols,
                                           KC, 0, 1, deps=[t_q],
                                           evac_deps=st_free)
            stores = []
            for (buf, hbm) in ((sth, pW["Nh"]), (stl, pW["Nl"])):
                in_a = buf[:, 0:D].rearrange("p (o c) -> p o c", c=512)
                out_a = bass.AP(hbm.tensor, kf * 512,
                                [[2 * 22 * 512, P], [22 * 512, 2], [1, 512]])
                stores.append(S.op("gp", lambda E, o=out_a, i=in_a:
                                   E.dma_start(out=o, in_=i),
                                   deps=evs, sig=True))
            st_free = stores
            prep_done += stores

    pes.close()

    u_tm = [A("u_tm%d" % i, [P, TS * D]) for i in range(2)]
    xn_tm = A("xn_tm", [P, TS * D])
    xnf_h = A("xnf_h", [P, KC * MT], BF16)
    xnf_l = A("xnf_l", [P, KC * MT], BF16)
    xqf_h = A("xqf_h", [P, KC * MT], BF16)
    xqf_l = A("xqf_l", [P, KC * MT], BF16)
    xsf = A("xsf", [P, KC * MT])
    xsf_h = A("xsf_h", [P, KC * MT], BF16)
    xsf_l = A("xsf_l", [P, KC * MT], BF16)
    h_tm = A("h_tm", [P, TS * I])
    hqf_h = A("hqf_h", [P, IC * MT], BF16)
    hqf_l = A("hqf_l", [P, IC * MT], BF16)
    g_tm = A("g_tm", [P, TS * D])
    NSLOT = 4
    ring_h = [A("rng_h%d" % i, [P, 8 * 512], BF16) for i in range(NSLOT)]
    ring_l = [A("rng_l%d" % i, [P, 8 * 512], BF16) for i in range(NSLOT)]

    ring_free = [[prep_done[-1]] for _ in range(NSLOT)]
    ring_idx = [0]

    def stream_chunk(key, c0, ncols):
        sl = ring_idx[0] % NSLOT
        ring_idx[0] += 1
        th = S.op("sp", lambda E, sl=sl: E.dma_start(
            out=ring_h[sl][:, 0:ncols], in_=pW[key + "h"][:, c0:c0 + ncols]),
            deps=ring_free[sl], sig=True)
        tl = S.op("gp", lambda E, sl=sl: E.dma_start(
            out=ring_l[sl][:, 0:ncols], in_=pW[key + "l"][:, c0:c0 + ncols]),
            deps=ring_free[sl], sig=True)
        ring_free[sl] = []
        return sl, th, tl

    def slot_consumed(sl, tok):
        ring_free[sl].append(tok)

    prep_fin = prep_done[-1]
    st_m = {
        "u": [prep_fin, prep_fin],
        "scan_prev": t_init,
        "state_ok": t_init,
        "xn_free": prep_fin,
        "xnf_free": prep_fin,
        "xqf_free": prep_fin,
        "xs_free": prep_fin,
        "xsfhl_free": prep_fin,
        "h_free": prep_fin,
        "hqf_free": prep_fin,
        "g_free": prep_fin,
    }
    C = {}

    sc_d, dc_d = [], []
    for k in range(KC):
        for j in range(TS):
            sc_d.append(j * D + k * P)
            dc_d.append(k * MT + j * P)
    sc_h, dc_h = [], []
    for kf in range(IC):
        for j in range(TS):
            sc_h.append(j * I + kf * P)
            dc_h.append(kf * MT + j * P)

    def stage_LD(m, warm):
        buf = m % 2
        row0 = m * MT if warm else T_WARM + m * MT
        uin_v = u_in[row0:row0 + MT, :].rearrange("(j p) f -> p j f", p=P)
        uout_v = u_tm[buf][:, 0:TS * D].rearrange("p (j f) -> p j f", f=D)
        C[(m, warm, "ld")] = S.op("gp", lambda E: E.dma_start(
            out=uout_v, in_=uin_v), deps=[st_m["u"][buf]], sig=True)

    def stage_LN(m, warm):
        buf = m % 2
        t_ld = C[(m, warm, "ld")]
        t_ln = []
        for j in range(TS):
            uj = u_tm[buf][:, j * D:(j + 1) * D]
            ts1 = S.op("dve", lambda E, uj=uj, j=j: E.tensor_reduce(
                out=lnsum[:, j:j + 1], in_=uj, axis=AX.X, op=ALU.add),
                deps=[t_ld], sig=True)
            ts2 = S.op("act", lambda E, uj=uj, j=j: E.activation(
                out=lnsq[:, 0:D], in_=uj, func=AF.Square,
                accum_out=lnv[:, j:j + 1]), deps=[t_ld], sig=True)
            ts3 = S.op("dve", lambda E, j=j: E.tensor_scalar(
                out=lnm[:, j:j + 1], in0=lnsum[:, j:j + 1], scalar1=1.0 / D,
                scalar2=None, op0=ALU.mult), deps=[ts1], sig=True)
            ts4 = S.op("dve", lambda E, j=j: E.tensor_scalar(
                out=lnv[:, j:j + 1], in0=lnv[:, j:j + 1], scalar1=1.0 / D,
                scalar2=None, op0=ALU.mult), deps=[ts2, ts3], sig=True)
            ts5 = S.op("dve", lambda E, j=j: E.scalar_tensor_tensor(
                out=lnv[:, j:j + 1], in0=lnm[:, j:j + 1],
                scalar=lnm[:, j:j + 1], in1=lnv[:, j:j + 1], op0=ALU.mult,
                op1=ALU.subtract), deps=[ts4], sig=True)
            ts6 = S.op("dve", lambda E, j=j: E.tensor_scalar(
                out=lnv[:, j:j + 1], in0=lnv[:, j:j + 1], scalar1=-1.0,
                scalar2=1e-5, op0=ALU.mult, op1=ALU.add), deps=[ts5], sig=True)
            ts7 = S.op("act", lambda E, j=j: E.activation(
                out=lnr[:, j:j + 1], in_=lnv[:, j:j + 1], func=AF.Sqrt),
                deps=[ts6], sig=True)
            ts8 = S.op("dve", lambda E, j=j: E.reciprocal(
                out=lnr[:, j:j + 1], in_=lnr[:, j:j + 1]), deps=[ts7], sig=True)
            t9 = S.op("dve", lambda E, uj=uj, j=j: E.tensor_scalar(
                out=xn_tm[:, j * D:(j + 1) * D], in0=uj,
                scalar1=lnm[:, j:j + 1], scalar2=lnr[:, j:j + 1],
                op0=ALU.subtract, op1=ALU.mult),
                deps=[ts8, st_m["xn_free"]], sig=True)
            t_ln.append(t9)
        C[(m, warm, "ln")] = t_ln

    def stage_TXN(m, warm):
        tpe, evs = emit_transpose_evac(
            xn_tm, sc_d, xnf_h, xnf_l, dc_d, KC * TS, 0, 1,
            deps=C[(m, warm, "ln")], evac_deps=[st_m["xnf_free"]])
        C[(m, warm, "txn_pe")] = tpe
        C[(m, warm, "txn_ev")] = evs

    def stage_MMB(m, warm):
        evs_xn = C[(m, warm, "txn_ev")]
        t_mmb = []
        for half in range(2):
            sl, th, tl = stream_chunk("B", half * 4096, 4096)
            for o4 in range(4):
                oc = half * 4 + o4
                bank = oc // 2
                part = oc % 2
                ps = pb[bank][:, part * MT:(part + 1) * MT]
                pairs = []
                for k in range(KC):
                    pairs.append((
                        ring_h[sl][:, k * 512 + o4 * P:k * 512 + (o4 + 1) * P],
                        ring_l[sl][:, k * 512 + o4 * P:k * 512 + (o4 + 1) * P],
                        xnf_h[:, k * MT:(k + 1) * MT],
                        xnf_l[:, k * MT:(k + 1) * MT]))
                d = [th, tl, evs_xn[-1]] + bank_dep(bank)
                emit_mm3(ps, pairs, True, False, False, d)
                S.op("pe", lambda E, ps=ps, oc=oc: E.matmul(
                    ps, bBh[:, oc * P:(oc + 1) * P], ones_bf[:, 0:MT],
                    start=False, stop=False), deps=[t_bias, t_ones])
                tok = S.op("pe", lambda E, ps=ps, oc=oc: E.matmul(
                    ps, bBl[:, oc * P:(oc + 1) * P], ones_bf[:, 0:MT],
                    start=False, stop=True), sig=True)
                t_mmb.append(tok)
            slot_consumed(sl, t_mmb[-1])
        C[(m, warm, "mmb")] = t_mmb

    def stage_SCAN(m, warm):
        t_mmb = C[(m, warm, "mmb")]
        t_scan = []
        for k in range(KC):
            bank = k // 2
            part = k % 2
            d0 = abar[:, k:k + 1].broadcast_to([P, MT])
            t_sc = S.op("dve", lambda E, k=k, bank=bank, part=part, d0=d0:
                        E.tensor_tensor_scan(
                out=xsf[:, k * MT:(k + 1) * MT], data0=d0,
                data1=pb[bank][:, part * MT:(part + 1) * MT],
                initial=state[:, k:k + 1], op0=ALU.mult, op1=ALU.add),
                deps=[t_mmb[2 * bank + 1], t_abar, st_m["state_ok"],
                      st_m["scan_prev"], st_m["xs_free"]], sig=True)
            t_scan.append(t_sc)
        t_cp = None
        for k in range(KC):
            t_cp = S.op("dve", lambda E, k=k: E.tensor_copy(
                out=state[:, k:k + 1],
                in_=xsf[:, (k + 1) * MT - 1:(k + 1) * MT]),
                deps=[t_scan[k]], sig=True)
        st_m["state_ok"] = t_cp
        st_m["scan_prev"] = t_cp
        for k in range(KC):
            pb_free[k // 2].append(t_scan[k])

    def stage_XSPLIT(m):
        txh = S.op("act", lambda E: E.activation(out=xsf_h[:, :], in_=xsf[:, :],
                                                 func=AF.Copy),
                   deps=[st_m["scan_prev"], st_m["xsfhl_free"]], sig=True)
        txl = S.op("dve", lambda E: E.tensor_tensor(
            out=xsf_l[:, :], in0=xsf[:, :], in1=xsf_h[:, :], op=ALU.subtract),
            deps=[txh], sig=True)
        st_m["xs_free"] = txl
        C[(m, False, "xsplit")] = txl

    def stage_QX(m):
        t_q = []
        for j in range(TS):
            t_q.append(emit_quant(xn_tm[:, j * D:(j + 1) * D],
                                  xn_tm[:, j * D:(j + 1) * D], D,
                                  [C[(m, False, "ln")][j],
                                   C[(m, False, "txn_pe")]]))
        C[(m, False, "qx")] = t_q

    def stage_TXQ(m):
        tpe, evs = emit_transpose_evac(
            xn_tm, sc_d, xqf_h, xqf_l, dc_d, KC * TS, 0, 1,
            deps=C[(m, False, "qx")], evac_deps=[st_m["xqf_free"]])
        st_m["xn_free"] = tpe
        C[(m, False, "txq_ev")] = evs

    def stage_GU(m):
        evs_xq = C[(m, False, "txq_ev")]
        t_mul = []
        last_gu = None
        for oc in range(6):
            w = 512 if oc < 5 else 256
            slg, thg, tlg = stream_chunk("G", oc * 4096, KC * w)
            slu, thu, tlu = stream_chunk("U", oc * 4096, KC * w)
            for j in range(TS):
                bkg, bku = 4 + j, 6 + j
                pairsg, pairsu = [], []
                for k in range(KC):
                    lh = xqf_h[:, k * MT + j * P:k * MT + (j + 1) * P]
                    ll = xqf_l[:, k * MT + j * P:k * MT + (j + 1) * P]
                    pairsg.append((lh, ll, ring_h[slg][:, k * w:(k + 1) * w],
                                   ring_l[slg][:, k * w:(k + 1) * w]))
                    pairsu.append((lh, ll, ring_h[slu][:, k * w:(k + 1) * w],
                                   ring_l[slu][:, k * w:(k + 1) * w]))
                tg = emit_mm3(pb[bkg][:, 0:w], pairsg, True, True, True,
                              [thg, tlg, evs_xq[-1]] + bank_dep(bkg))
                tu = emit_mm3(pb[bku][:, 0:w], pairsu, True, True, True,
                              [thu, tlu] + bank_dep(bku))
                last_gu = tu
                hcol = j * I + oc * 512
                tsil = S.op("act", lambda E, b=bkg, hcol=hcol, w=w: E.activation(
                    out=h_tm[:, hcol:hcol + w], in_=pb[b][:, 0:w], func=AF.Silu),
                    deps=[tg, st_m["h_free"]], sig=True)
                tm = S.op("dve", lambda E, b=bku, hcol=hcol, w=w: E.tensor_tensor(
                    out=h_tm[:, hcol:hcol + w], in0=h_tm[:, hcol:hcol + w],
                    in1=pb[b][:, 0:w], op=ALU.mult), deps=[tsil, tu], sig=True)
                t_mul.append(tm)
                pb_free[bkg].append(tsil)
                pb_free[bku].append(tm)
            slot_consumed(slg, t_mul[-1])
            slot_consumed(slu, t_mul[-1])
        st_m["xqf_free"] = last_gu
        C[(m, False, "mul")] = t_mul

    def stage_QH(m):
        t_qh = []
        for j in range(TS):
            for half in range(2):
                c0 = j * I + half * LW
                t_qh.append(emit_quant(h_tm[:, c0:c0 + LW], h_tm[:, c0:c0 + LW],
                                       LW, C[(m, False, "mul")]))
        C[(m, False, "qh")] = t_qh

    def stage_CD(m):
        txl = C[(m, False, "xsplit")]
        t_cd = {}
        for oc in range(2):
            slc, thc, tlc = stream_chunk("C", oc * 4096, 4096)
            sld, thd, tld = stream_chunk("D", oc * 4096, 4096)
            for j in range(TS):
                bank = 4 + j * 2 + oc
                ps = pb[bank][:, 0:512]
                pairs = []
                for k in range(KC):
                    pairs.append((
                        xsf_h[:, k * MT + j * P:k * MT + (j + 1) * P],
                        xsf_l[:, k * MT + j * P:k * MT + (j + 1) * P],
                        ring_h[slc][:, k * 512:(k + 1) * 512],
                        ring_l[slc][:, k * 512:(k + 1) * 512]))
                emit_mm3(ps, pairs, True, False, False,
                         [thc, tlc, txl] + bank_dep(bank))
                pairs = []
                for k in range(KC):
                    pairs.append((
                        xnf_h[:, k * MT + j * P:k * MT + (j + 1) * P],
                        xnf_l[:, k * MT + j * P:k * MT + (j + 1) * P],
                        ring_h[sld][:, k * 512:(k + 1) * 512],
                        ring_l[sld][:, k * 512:(k + 1) * 512]))
                emit_mm3(ps, pairs, False, False, False, [thd, tld])
                for bi, (bv, stp) in enumerate(((bCh, False), (bCl, False),
                                                (bDh, False), (bDl, True))):
                    tok = S.op("pe", lambda E, ps=ps, oc=oc, bv=bv, stp=stp:
                               E.matmul(
                        ps, ones_bf[:, 0:P], bv[:, oc * 512:(oc + 1) * 512],
                        start=False, stop=stp), deps=[t_bias] if bi == 0 else (),
                        sig=stp)
                t_cd[(j, oc)] = tok
            slot_consumed(slc, t_cd[(TS - 1, oc)])
            slot_consumed(sld, t_cd[(TS - 1, oc)])
        st_m["xnf_free"] = t_cd[(TS - 1, 1)]
        st_m["xsfhl_free"] = t_cd[(TS - 1, 1)]
        C[(m, False, "cd")] = t_cd

    def stage_THQ(m):
        tpe, evs = emit_transpose_evac(
            h_tm, sc_h, hqf_h, hqf_l, dc_h, IC * TS, 0, 1,
            deps=C[(m, False, "qh")], evac_deps=[st_m["hqf_free"]])
        st_m["h_free"] = tpe
        C[(m, False, "thq_ev")] = evs

    def stage_DWN(m):
        evs_hq = C[(m, False, "thq_ev")]
        t_down = {}
        tok = None
        for oc in range(2):
            for pi, (k0, nk) in enumerate(DN_PIECES):
                sl, th, tl = stream_chunk("N", (oc * 22 + k0) * 512, nk * 512)
                for j in range(TS):
                    bank = 2 + j
                    ps = pb[bank][:, 0:512]
                    pairs = []
                    for kk in range(nk):
                        kf = k0 + kk
                        pairs.append((
                            hqf_h[:, kf * MT + j * P:kf * MT + (j + 1) * P],
                            hqf_l[:, kf * MT + j * P:kf * MT + (j + 1) * P],
                            ring_h[sl][:, kk * 512:(kk + 1) * 512],
                            ring_l[sl][:, kk * 512:(kk + 1) * 512]))
                    d = [th, tl, evs_hq[-1]]
                    if pi == 0:
                        d += bank_dep(bank)
                    tok = emit_mm3(ps, pairs, pi == 0,
                                   pi == len(DN_PIECES) - 1, True, d)
                slot_consumed(sl, tok)
            for j in range(TS):
                bank = 2 + j
                gcol = j * D + oc * 512
                tg2 = S.op("act", lambda E, b=bank, gcol=gcol: E.activation(
                    out=g_tm[:, gcol:gcol + 512], in_=pb[b][:, 0:512],
                    func=AF.Silu), deps=[tok, st_m["g_free"]], sig=True)
                t_down[(j, oc)] = tg2
                pb_free[bank].append(tg2)
        st_m["hqf_free"] = tok
        C[(m, False, "dwn")] = t_down

    def stage_OUT(m):
        buf = m % 2
        t_cd = C[(m, False, "cd")]
        t_down = C[(m, False, "dwn")]
        t_comb = None
        for j in range(TS):
            for oc in range(2):
                bank = 4 + j * 2 + oc
                gcol = j * D + oc * 512
                t1 = S.op("dve", lambda E, b=bank, gcol=gcol: E.tensor_tensor(
                    out=g_tm[:, gcol:gcol + 512], in0=g_tm[:, gcol:gcol + 512],
                    in1=pb[b][:, 0:512], op=ALU.mult),
                    deps=[t_down[(j, oc)], t_cd[(j, oc)]], sig=True)
                t_comb = S.op("dve", lambda E, gcol=gcol, buf=buf: E.tensor_tensor(
                    out=u_tm[buf][:, gcol:gcol + 512],
                    in0=u_tm[buf][:, gcol:gcol + 512],
                    in1=g_tm[:, gcol:gcol + 512], op=ALU.add),
                    deps=[t1], sig=True)
                pb_free[bank].append(t_comb)
        st_m["g_free"] = t_comb
        orow = m * MT
        o_v = out_d[orow:orow + MT, :].rearrange("(j p) f -> p j f", p=P)
        i_v = u_tm[buf][:, 0:TS * D].rearrange("p (j f) -> p j f", f=D)
        t_o = S.op("gp", lambda E: E.dma_start(out=o_v, in_=i_v),
                   deps=[t_comb], sig=True)
        st_m["u"][buf] = t_o

    def front(m, warm):
        stage_LD(m, warm)
        stage_LN(m, warm)
        stage_TXN(m, warm)
        stage_MMB(m, warm)
        stage_SCAN(m, warm)
        if warm:
            st_m["u"][m % 2] = C[(m, warm, "ln")][-1]
            st_m["xnf_free"] = C[(m, warm, "mmb")][-1]
        else:
            stage_XSPLIT(m)
            stage_QX(m)
            stage_TXQ(m)

    for wm in range(NWARM):
        front(wm, True)
    st_m["state_ok"] = S.op("dve", lambda E: E.tensor_scalar(
        out=state[:, :], in0=state[:, :], scalar1=wsc[:, 0:1], scalar2=None,
        op0=ALU.mult), deps=[st_m["state_ok"], t_const], sig=True)
    st_m["scan_prev"] = st_m["state_ok"]

    front(0, False)
    for m in range(NMAC):
        stage_GU(m)
        if m + 1 < NMAC:
            stage_LD(m + 1, False)
            stage_LN(m + 1, False)
        stage_QH(m)
        stage_CD(m)
        if m + 1 < NMAC:
            stage_TXN(m + 1, False)
        stage_THQ(m)
        if m + 1 < NMAC:
            stage_QX(m + 1)
        stage_DWN(m)
        stage_OUT(m)
        if m + 1 < NMAC:
            stage_MMB(m + 1, False)
            stage_SCAN(m + 1, False)
            stage_XSPLIT(m + 1)
            stage_TXQ(m + 1)

    S.op("gp", lambda E: E.wait_ge(S.sem["gp"], S.cnt["gp"]))
    S.op("sp", lambda E: E.wait_ge(S.sem["sp"], S.cnt["sp"]))

    with nc.Block() as block:
        @block.tensor
        def _(E):
            for fn in S.q["pe"]:
                fn(E)

        @block.vector
        def _(E):
            for fn in S.q["dve"]:
                fn(E)

        @block.scalar
        def _(E):
            for fn in S.q["act"]:
                fn(E)

        @block.sync
        def _(E):
            for fn in S.q["sp"]:
                fn(E)

        @block.gpsimd
        def _(E):
            for fn in S.q["gp"]:
                fn(E)

    es.close()
    return nc


def kernel(**inputs):
    u = np.asarray(inputs["L0_attn_u"], dtype=np.float32)
    log_A = np.asarray(inputs["L0_attn_log_A"], dtype=np.float32)
    W_B = np.asarray(inputs["L0_attn_W_B"], dtype=np.float32)
    b_B = np.asarray(inputs["L0_attn_b_B"], dtype=np.float32)
    W_C = np.asarray(inputs["L0_attn_W_C"], dtype=np.float32)
    b_C = np.asarray(inputs["L0_attn_b_C"], dtype=np.float32)
    W_D = np.asarray(inputs["L0_attn_W_D"], dtype=np.float32)
    b_D = np.asarray(inputs["L0_attn_b_D"], dtype=np.float32)
    log_dt = np.asarray(inputs["L0_attn_log_dt"], dtype=np.float32)
    w_gate = np.asarray(inputs["L0_ffn_w_gate"], dtype=np.float32)
    w_up = np.asarray(inputs["L0_ffn_w_up"], dtype=np.float32)
    w_down = np.asarray(inputs["L0_ffn_w_down"], dtype=np.float32)

    WBt = np.ascontiguousarray(W_B.T)
    WCt = np.ascontiguousarray(W_C.T)
    WDt = np.ascontiguousarray(W_D.T)
    logA_t = np.ascontiguousarray(log_A.reshape(KC, P).T)
    ident = np.eye(P, dtype=np.float32)

    nc = build_program()
    in_maps = []
    for c in range(8):
        b, q = c // 4, c % 4
        t0 = q * T_CORE
        wstart = t0 - T_WARM if q > 0 else 0
        u_c = np.ascontiguousarray(
            np.concatenate([u[b, wstart:wstart + T_WARM], u[b, t0:t0 + T_CORE]],
                           axis=0))
        wsc_v = np.full((P, 1), 1.0 if q > 0 else 0.0, dtype=np.float32)
        in_maps.append({
            "u": u_c,
            "wbt": WBt, "wct": WCt, "wdt": WDt,
            "wg": w_gate, "wu": w_up, "wdn": w_down,
            "logA": logA_t, "ldt": log_dt.reshape(1, 1),
            "wsc": wsc_v,
            "bB": b_B.reshape(1, D), "bC": b_C.reshape(1, D),
            "bD": b_D.reshape(1, D), "ident": ident,
        })
    res = run_bass_kernel_spmd(nc, in_maps, core_ids=list(range(8)))
    out = np.empty((2, 16384, D), dtype=np.float32)
    for c in range(8):
        b, q = c // 4, c % 4
        out[b, q * T_CORE:(q + 1) * T_CORE] = res.results[c]["out"]
    return out


# version 04
# speedup vs baseline: 1.2799x; 1.0289x over previous
import contextlib
import numpy as np
import concourse.bass as bass
import concourse.mybir as mybir
from concourse.bass_utils import run_bass_kernel_spmd

F32 = mybir.dt.float32
BF16 = mybir.dt.bfloat16
AF = mybir.ActivationFunctionType
ALU = mybir.AluOpType
AX = mybir.AxisListType

P = 128
D = 1024
I = 2816
KC = D // P
IC = I // P
TS = 2
MT = TS * P
NWARM = 4
NMAC = 16
T_CORE = NMAC * MT
T_WARM = NWARM * MT
CLAMP = float(np.float32(np.float32(6.0) * np.float32(1e-12)))
LW = 1408
DN_PIECES = ((0, 8), (8, 8), (16, 6))

ENGINES = ("pe", "dve", "act", "sp", "gp")


class Sched:
    def __init__(self, nc, es):
        self.nc = nc
        self.q = {e: [] for e in ENGINES}
        self.cnt = {e: 0 for e in ENGINES}
        self.sem = {e: es.enter_context(nc.semaphore("S_" + e)) for e in ENGINES}
        self.waited = {}

    def op(self, eng, fn, deps=(), sig=False, step=None):
        for tok in deps:
            if tok is None:
                continue
            peng, c = tok
            key = (eng, peng)
            if self.waited.get(key, 0) >= c:
                continue
            self.waited[key] = c
            s = self.sem[peng]
            self.q[eng].append(lambda E, s=s, c=c: E.wait_ge(s, c))
        if sig:
            if step is None:
                step = 16 if eng in ("sp", "gp") else 1
            self.cnt[eng] += step
            s = self.sem[eng]

            def run(E, fn=fn, s=s, step=step):
                fn(E).then_inc(s, step)
            self.q[eng].append(run)
            return (eng, self.cnt[eng])
        self.q[eng].append(lambda E, fn=fn: fn(E))
        return None


def build_program():
    nc = bass.Bass("TRN2", target_bir_lowering=False)
    es = contextlib.ExitStack()
    S = Sched(nc, es)

    u_in = nc.dram_tensor("u", [T_WARM + T_CORE, D], F32, kind="ExternalInput").ap()
    out_d = nc.dram_tensor("out", [T_CORE, D], F32, kind="ExternalOutput").ap()
    rWB = nc.dram_tensor("wbt", [D, D], F32, kind="ExternalInput").ap()
    rWC = nc.dram_tensor("wct", [D, D], F32, kind="ExternalInput").ap()
    rWD = nc.dram_tensor("wdt", [D, D], F32, kind="ExternalInput").ap()
    rWG = nc.dram_tensor("wg", [I, D], F32, kind="ExternalInput").ap()
    rWU = nc.dram_tensor("wu", [I, D], F32, kind="ExternalInput").ap()
    rWN = nc.dram_tensor("wdn", [D, I], F32, kind="ExternalInput").ap()
    logA_in = nc.dram_tensor("logA", [P, KC], F32, kind="ExternalInput").ap()
    ldt_in = nc.dram_tensor("ldt", [1, 1], F32, kind="ExternalInput").ap()
    wsc_in = nc.dram_tensor("wsc", [P, 1], F32, kind="ExternalInput").ap()
    bB_in = nc.dram_tensor("bB", [1, D], F32, kind="ExternalInput").ap()
    bC_in = nc.dram_tensor("bC", [1, D], F32, kind="ExternalInput").ap()
    bD_in = nc.dram_tensor("bD", [1, D], F32, kind="ExternalInput").ap()
    ident_in = nc.dram_tensor("ident", [P, P], F32, kind="ExternalInput").ap()
    pW = {}
    for nm, cols in (("B", 2 * 4096), ("C", 2 * 4096), ("D", 2 * 4096),
                     ("G", 6 * 4096), ("U", 6 * 4096), ("N", 2 * 22 * 512)):
        pW[nm + "h"] = nc.dram_tensor("pW%s_h" % nm, [P, cols], BF16).ap()
        pW[nm + "l"] = nc.dram_tensor("pW%s_l" % nm, [P, cols], BF16).ap()

    A = lambda n, s, dt=F32: nc.alloc_sbuf_tensor(n, s, dt).ap()
    ident = A("identt", [P, P])
    logA = A("logAt", [P, KC])
    ldt = A("ldtt", [1, 1])
    wsc = A("wsct", [P, 1])
    dtv = A("dtv", [1, 1])
    dt_bc = A("dt_bc", [P, 1])
    abar = A("abar", [P, KC])
    atmp = A("atmp", [P, KC])
    onesf = A("onesf", [1, P])
    ones_bf = A("ones_bf", [1, MT], BF16)
    bBh = A("bBh", [1, D], BF16)
    bBl = A("bBl", [1, D], BF16)
    bCh = A("bCh", [1, D], BF16)
    bCl = A("bCl", [1, D], BF16)
    bDh = A("bDh", [1, D], BF16)
    bDl = A("bDl", [1, D], BF16)
    state = A("state", [P, KC])
    lad1 = A("lad1", [P, LW])
    lad2 = A("lad2", [P, LW])
    lad3 = A("lad3", [P, LW])
    scm = A("scm", [P, LW // 16])
    scr = A("scr", [P, LW // 16])
    scs = A("scs", [P, LW // 16])
    lnsum = A("lnsum", [P, TS])
    lnsq = A("lnsq", [P, D])
    lnm = A("lnm", [P, TS])
    lnv = A("lnv", [P, TS])
    lnr = A("lnr", [P, TS])

    pb = [nc.alloc_psum_tensor("pb%d" % i, [P, 512], F32).ap() for i in range(8)]

    pes = contextlib.ExitStack()
    bB = pes.enter_context(nc.sbuf_tensor("bBt", [1, D], F32)).ap()
    bC = pes.enter_context(nc.sbuf_tensor("bCt", [1, D], F32)).ap()
    bD = pes.enter_context(nc.sbuf_tensor("bDt", [1, D], F32)).ap()

    t_const = None
    for (dst, src) in ((ident, ident_in), (logA, logA_in), (ldt, ldt_in),
                       (wsc, wsc_in), (bB, bB_in), (bC, bC_in), (bD, bD_in)):
        t_const = S.op("gp", lambda E, d=dst, s=src: E.dma_start(out=d[:], in_=s),
                       sig=True)
    t_ones = S.op("gp", lambda E: E.memset(ones_bf[:], 1.0), sig=True, step=1)
    t_onesf = S.op("gp", lambda E: E.memset(onesf[:], 1.0), sig=True, step=1)
    t_init = S.op("gp", lambda E: E.memset(state[:], 0.0), sig=True, step=1)

    t_dtv = S.op("act", lambda E: E.activation(out=dtv[:], in_=ldt[:], func=AF.Exp),
                 deps=[t_const], sig=True)
    t_mmdt = S.op("pe", lambda E: E.matmul(pb[0][:, 0:1], onesf[:], dtv[:],
                                           start=True, stop=True),
                  deps=[t_dtv, t_onesf], sig=True)
    t_dtbc = S.op("dve", lambda E: E.tensor_copy(out=dt_bc[:], in_=pb[0][:, 0:1]),
                  deps=[t_mmdt], sig=True)
    t_e1 = S.op("act", lambda E: E.activation(out=atmp[:], in_=logA[:], func=AF.Exp),
                deps=[t_const], sig=True)
    t_e2 = S.op("dve", lambda E: E.tensor_scalar(out=atmp[:], in0=atmp[:],
                                                 scalar1=dt_bc[:, 0:1],
                                                 scalar2=-1.0, op0=ALU.mult,
                                                 op1=ALU.mult),
                deps=[t_e1, t_dtbc], sig=True)
    t_abar = S.op("act", lambda E: E.activation(out=abar[:], in_=atmp[:], func=AF.Exp),
                  deps=[t_e2], sig=True)
    t_bbdt = S.op("dve", lambda E: E.tensor_scalar(out=bB[:], in0=bB[:],
                                                   scalar1=dt_bc[0:1, 0:1],
                                                   scalar2=None, op0=ALU.mult),
                  deps=[t_const, t_dtbc], sig=True)
    t_bias = None
    for (src, hi, lo, dep) in ((bB, bBh, bBl, t_bbdt), (bC, bCh, bCl, t_const),
                               (bD, bDh, bDl, t_const)):
        S.op("dve", lambda E, s=src, h=hi: E.tensor_copy(out=h[:], in_=s[:]),
             deps=[dep])
        t_bias = S.op("dve", lambda E, s=src, h=hi, l=lo: E.tensor_tensor(
            out=l[:], in0=s[:], in1=h[:], op=ALU.subtract), sig=True)

    pb_free = {i: [] for i in range(8)}
    pb_free[0].append(t_dtbc)

    def bank_dep(i):
        d = pb_free[i]
        pb_free[i] = []
        return d

    def emit_quant(src_ap, dst_ap, Fw, deps):
        Nb = Fw // 16
        t_abs = S.op("act", lambda E: E.activation(out=lad1[:, 0:Fw], in_=src_ap,
                                                   func=AF.Abs),
                     deps=list(deps), sig=True)
        srcv = src_ap.rearrange("p (b k) -> p b k", k=16)
        t_lvl = S.op("dve", lambda E: E.tensor_reduce(
            out=scm[:, 0:Nb].unsqueeze(2), in_=srcv, axis=AX.X, op=ALU.max,
            apply_absolute_value=True), deps=list(deps), sig=True)
        t1 = S.op("dve", lambda E: E.tensor_scalar(
            out=scm[:, 0:Nb], in0=scm[:, 0:Nb], scalar1=CLAMP,
            scalar2=None, op0=ALU.max), deps=[t_lvl], sig=True)
        t2 = S.op("dve", lambda E: E.reciprocal(out=scr[:, 0:Nb], in_=scm[:, 0:Nb]),
                  deps=[t1], sig=True)
        t3 = S.op("dve", lambda E: E.tensor_scalar(
            out=scr[:, 0:Nb], in0=scr[:, 0:Nb], scalar1=6.0, scalar2=None,
            op0=ALU.mult), deps=[t2], sig=True)
        t4 = S.op("dve", lambda E: E.reciprocal(out=scs[:, 0:Nb], in_=scr[:, 0:Nb]),
                  deps=[t3], sig=True)
        r6b = scr[:, 0:Nb].unsqueeze(2).broadcast_to([P, Nb, 16])
        l1v = lad1[:, 0:Fw].rearrange("p (b k) -> p b k", k=16)
        t5 = S.op("dve", lambda E: E.tensor_tensor(out=l1v, in0=l1v, in1=r6b,
                                                   op=ALU.mult),
                  deps=[t4, t_abs], sig=True)
        S.op("dve", lambda E: E.tensor_scalar(out=lad2[:, 0:Fw], in0=lad1[:, 0:Fw],
                                              scalar1=0.25, scalar2=None,
                                              op0=ALU.is_gt), deps=[t5])
        for th in (0.75, 1.25, 1.75):
            S.op("dve", lambda E, th=th: E.scalar_tensor_tensor(
                out=lad2[:, 0:Fw], in0=lad1[:, 0:Fw], scalar=th,
                in1=lad2[:, 0:Fw], op0=ALU.is_gt, op1=ALU.add))
        S.op("dve", lambda E: E.tensor_scalar(out=lad3[:, 0:Fw], in0=lad1[:, 0:Fw],
                                              scalar1=2.5, scalar2=None,
                                              op0=ALU.is_gt))
        for th in (3.5, 5.0, 5.0):
            S.op("dve", lambda E, th=th: E.scalar_tensor_tensor(
                out=lad3[:, 0:Fw], in0=lad1[:, 0:Fw], scalar=th,
                in1=lad3[:, 0:Fw], op0=ALU.is_gt, op1=ALU.add))
        t_qu = S.op("dve", lambda E: E.scalar_tensor_tensor(
            out=lad2[:, 0:Fw], in0=lad2[:, 0:Fw], scalar=0.5, in1=lad3[:, 0:Fw],
            op0=ALU.mult, op1=ALU.add), sig=True)
        t_sg = S.op("act", lambda E: E.activation(out=lad3[:, 0:Fw], in_=src_ap,
                                                  func=AF.Sign),
                    deps=[t_qu], sig=True)
        sb = scs[:, 0:Nb].unsqueeze(2).broadcast_to([P, Nb, 16])
        l3v = lad3[:, 0:Fw].rearrange("p (b k) -> p b k", k=16)
        t_ss = S.op("dve", lambda E: E.tensor_tensor(out=l3v, in0=l3v, in1=sb,
                                                     op=ALU.mult),
                    deps=[t_sg], sig=True)
        t_q = S.op("dve", lambda E: E.tensor_tensor(out=dst_ap, in0=lad2[:, 0:Fw],
                                                    in1=lad3[:, 0:Fw],
                                                    op=ALU.mult),
                   deps=[t_ss], sig=True)
        return t_q

    def emit_transpose_evac(src_tile, src_cols, dst_h, dst_l, dst_cols, nblk,
                            bank_a, bank_b, deps, evac_deps=()):
        toks = []
        prev_evac = {}
        t_pe = None
        for w0 in range(0, nblk, 4):
            nb = min(4, nblk - w0)
            bank = bank_a if (w0 // 4) % 2 == 0 else bank_b
            d = list(deps) + bank_dep(bank)
            if bank in prev_evac:
                d.append(prev_evac[bank])
            for i in range(nb):
                blk = w0 + i
                t_pe = S.op("pe", lambda E, b=bank, i=i, blk=blk: E.transpose(
                    pb[b][:, i * 128:(i + 1) * 128],
                    src_tile[:, src_cols[blk]:src_cols[blk] + 128], ident[:]),
                    deps=d if i == 0 else (), sig=(i == nb - 1))
            dc0 = dst_cols[w0]
            th = S.op("act", lambda E, b=bank, n=nb, dc0=dc0: E.activation(
                out=dst_h[:, dc0:dc0 + n * 128], in_=pb[b][:, 0:n * 128],
                func=AF.Copy), deps=[t_pe] + list(evac_deps), sig=True)
            tl = S.op("dve", lambda E, b=bank, n=nb, dc0=dc0: E.scalar_tensor_tensor(
                out=dst_l[:, dc0:dc0 + n * 128], in0=pb[b][:, 0:n * 128],
                scalar=1.0, in1=dst_h[:, dc0:dc0 + n * 128], op0=ALU.mult,
                op1=ALU.subtract), deps=[th] + list(evac_deps), sig=True)
            prev_evac[bank] = tl
            toks.append(tl)
        for b, tl in prev_evac.items():
            pb_free[b].append(tl)
        return t_pe, toks

    def emit_mm3(psum_ap, pairs, start, stop, sig, deps):
        ops = []
        for (lh, ll, rh, rl) in pairs:
            ops.append((lh, rh))
            ops.append((lh, rl))
            ops.append((ll, rh))
        tok = None
        for i, (l, r) in enumerate(ops):
            st = start and i == 0
            last = i == len(ops) - 1
            tok = S.op("pe", lambda E, l=l, r=r, st=st, sp=stop and last:
                       E.matmul(psum_ap, l, r, start=st, stop=sp),
                       deps=deps if i == 0 else (), sig=sig and last)
        return tok

    def hview(t, r0, nchunk, c0, w):
        return t[r0:r0 + nchunk * P, c0:c0 + w].rearrange("(k p) c -> p k c", p=P)

    with nc.sbuf_tensor("wqf", [P, 22 * D], F32) as wqf_t, \
         nc.sbuf_tensor("sth", [P, 8192], BF16) as sth_t, \
         nc.sbuf_tensor("stl", [P, 8192], BF16) as stl_t:
        wqf = wqf_t.ap()
        sth = sth_t.ap()
        stl = stl_t.ap()

        wqf_free = []
        st_free = []
        prep_done = []

        for (raw, ph, pl, scale_dt) in ((rWB, pW["Bh"], pW["Bl"], True),
                                        (rWC, pW["Ch"], pW["Cl"], False),
                                        (rWD, pW["Dh"], pW["Dl"], False)):
            t_ld = S.op("gp", lambda E, raw=raw: E.dma_start(
                out=wqf[:, 0:KC * D].rearrange("p (k c) -> p k c", c=D),
                in_=hview(raw, 0, KC, 0, D)), deps=wqf_free, sig=True)
            d = [t_ld]
            if scale_dt:
                d = [S.op("dve", lambda E: E.tensor_scalar(
                    out=wqf[:, 0:KC * D], in0=wqf[:, 0:KC * D],
                    scalar1=dt_bc[:, 0:1], scalar2=None, op0=ALU.mult),
                    deps=[t_ld, t_dtbc], sig=True)]
            t_h = S.op("dve", lambda E: E.tensor_copy(out=sth[:, 0:KC * D],
                                                      in_=wqf[:, 0:KC * D]),
                       deps=d + st_free, sig=True)
            t_l = S.op("dve", lambda E: E.tensor_tensor(
                out=stl[:, 0:KC * D], in0=wqf[:, 0:KC * D],
                in1=sth[:, 0:KC * D], op=ALU.subtract), deps=[t_h], sig=True)
            stores = []
            for oc in range(2):
                sb_v = sth[:, 0:KC * D].rearrange(
                    "p (k c) -> p k c", c=D)[:, :, oc * 512:(oc + 1) * 512]
                sb_vl = stl[:, 0:KC * D].rearrange(
                    "p (k c) -> p k c", c=D)[:, :, oc * 512:(oc + 1) * 512]
                hb = ph[:, oc * 4096:(oc + 1) * 4096].rearrange(
                    "p (k c) -> p k c", c=512)
                hbl = pl[:, oc * 4096:(oc + 1) * 4096].rearrange(
                    "p (k c) -> p k c", c=512)
                stores.append(S.op("gp", lambda E, o=hb, i=sb_v: E.dma_start(
                    out=o, in_=i), deps=[t_h], sig=True))
                stores.append(S.op("gp", lambda E, o=hbl, i=sb_vl: E.dma_start(
                    out=o, in_=i), deps=[t_l], sig=True))
            wqf_free = [t_l] + stores
            st_free = stores
            prep_done += stores

        for (raw, ph, pl) in ((rWG, pW["Gh"], pW["Gl"]), (rWU, pW["Uh"], pW["Ul"])):
            t_lds = []
            for part in range(2):
                t_lds.append(S.op("gp", lambda E, raw=raw, part=part: E.dma_start(
                    out=wqf[:, part * 11 * D:(part * 11 + 11) * D].rearrange(
                        "p (k c) -> p k c", c=D),
                    in_=hview(raw, part * 11 * P, 11, 0, D)),
                    deps=wqf_free, sig=True))
            t_q = None
            for j in range(IC):
                t_q = emit_quant(wqf[:, j * D:(j + 1) * D],
                                 wqf[:, j * D:(j + 1) * D], D,
                                 [t_lds[0] if j < 11 else t_lds[1]])
            wqf_free = []
            for kf in range(KC):
                src_cols = [j * D + kf * P for j in range(IC)]
                dst_cols = [j * P for j in range(IC)]
                tpe, evs = emit_transpose_evac(wqf, src_cols, sth, stl, dst_cols,
                                               IC, 0, 1, deps=[t_q],
                                               evac_deps=st_free)
                stores = []
                for (buf, hbm) in ((sth, ph), (stl, pl)):
                    in_a = buf[:, 0:5 * 512].rearrange("p (o c) -> p o c", c=512)
                    out_a = bass.AP(hbm.tensor, kf * 512,
                                    [[6 * 4096, P], [4096, 5], [1, 512]])
                    stores.append(S.op("gp", lambda E, o=out_a, i=in_a:
                                       E.dma_start(out=o, in_=i),
                                       deps=evs, sig=True))
                    in_b = buf[:, 5 * 512:5 * 512 + 256]
                    out_b = bass.AP(hbm.tensor, 5 * 4096 + kf * 256,
                                    [[6 * 4096, P], [1, 256]])
                    stores.append(S.op("gp", lambda E, o=out_b, i=in_b:
                                       E.dma_start(out=o, in_=i),
                                       deps=evs, sig=True))
                st_free = stores
                prep_done += stores
                wqf_free = stores + [tpe]

        t_ldn = S.op("gp", lambda E: E.dma_start(
            out=wqf[:, 0:KC * I].rearrange("p (k c) -> p k c", c=I),
            in_=hview(rWN, 0, KC, 0, I)), deps=wqf_free, sig=True)
        t_q = None
        for r in range(KC):
            for half in range(2):
                t_q = emit_quant(
                    wqf[:, r * I + half * LW:r * I + half * LW + LW],
                    wqf[:, r * I + half * LW:r * I + half * LW + LW], LW,
                    [t_ldn])
        for kf in range(IC):
            src_cols = [r * I + kf * P for r in range(KC)]
            dst_cols = [r * P for r in range(KC)]
            tpe, evs = emit_transpose_evac(wqf, src_cols, sth, stl, dst_cols,
                                           KC, 0, 1, deps=[t_q],
                                           evac_deps=st_free)
            stores = []
            for (buf, hbm) in ((sth, pW["Nh"]), (stl, pW["Nl"])):
                in_a = buf[:, 0:D].rearrange("p (o c) -> p o c", c=512)
                out_a = bass.AP(hbm.tensor, kf * 512,
                                [[2 * 22 * 512, P], [22 * 512, 2], [1, 512]])
                stores.append(S.op("gp", lambda E, o=out_a, i=in_a:
                                   E.dma_start(out=o, in_=i),
                                   deps=evs, sig=True))
            st_free = stores
            prep_done += stores

    pes.close()

    u_tm = [A("u_tm%d" % i, [P, TS * D]) for i in range(2)]
    xn_tm = A("xn_tm", [P, TS * D])
    xnf_h = A("xnf_h", [P, KC * MT], BF16)
    xnf_l = A("xnf_l", [P, KC * MT], BF16)
    xqf_h = A("xqf_h", [P, KC * MT], BF16)
    xqf_l = A("xqf_l", [P, KC * MT], BF16)
    xsf = A("xsf", [P, KC * MT])
    xsf_h = A("xsf_h", [P, KC * MT], BF16)
    xsf_l = A("xsf_l", [P, KC * MT], BF16)
    h_tm = A("h_tm", [P, TS * I])
    hqf_h = A("hqf_h", [P, IC * MT], BF16)
    hqf_l = A("hqf_l", [P, IC * MT], BF16)
    g_tm = A("g_tm", [P, TS * D])
    NSLOT = 4
    ring_h = [A("rng_h%d" % i, [P, 8 * 512], BF16) for i in range(NSLOT)]
    ring_l = [A("rng_l%d" % i, [P, 8 * 512], BF16) for i in range(NSLOT)]

    ring_free = [[prep_done[-1]] for _ in range(NSLOT)]
    ring_idx = [0]

    def stream_chunk(key, c0, ncols):
        sl = ring_idx[0] % NSLOT
        ring_idx[0] += 1
        th = S.op("sp", lambda E, sl=sl: E.dma_start(
            out=ring_h[sl][:, 0:ncols], in_=pW[key + "h"][:, c0:c0 + ncols]),
            deps=ring_free[sl], sig=True)
        tl = S.op("gp", lambda E, sl=sl: E.dma_start(
            out=ring_l[sl][:, 0:ncols], in_=pW[key + "l"][:, c0:c0 + ncols]),
            deps=ring_free[sl], sig=True)
        ring_free[sl] = []
        return sl, th, tl

    def slot_consumed(sl, tok):
        ring_free[sl].append(tok)

    prep_fin = prep_done[-1]
    st_m = {
        "u": [prep_fin, prep_fin],
        "scan_prev": t_init,
        "state_ok": t_init,
        "xn_free": prep_fin,
        "xnf_free": prep_fin,
        "xqf_free": prep_fin,
        "xs_free": prep_fin,
        "xsfhl_free": prep_fin,
        "h_free": prep_fin,
        "hqf_free": prep_fin,
        "g_free": prep_fin,
    }
    C = {}

    sc_d, dc_d = [], []
    for k in range(KC):
        for j in range(TS):
            sc_d.append(j * D + k * P)
            dc_d.append(k * MT + j * P)
    sc_h, dc_h = [], []
    for kf in range(IC):
        for j in range(TS):
            sc_h.append(j * I + kf * P)
            dc_h.append(kf * MT + j * P)

    def stage_LD(m, warm):
        buf = m % 2
        row0 = m * MT if warm else T_WARM + m * MT
        uin_v = u_in[row0:row0 + MT, :].rearrange("(j p) f -> p j f", p=P)
        uout_v = u_tm[buf][:, 0:TS * D].rearrange("p (j f) -> p j f", f=D)
        C[(m, warm, "ld")] = S.op("gp", lambda E: E.dma_start(
            out=uout_v, in_=uin_v), deps=[st_m["u"][buf]], sig=True)

    def stage_LN(m, warm):
        buf = m % 2
        t_ld = C[(m, warm, "ld")]
        t_ln = []
        for j in range(TS):
            uj = u_tm[buf][:, j * D:(j + 1) * D]
            ts1 = S.op("dve", lambda E, uj=uj, j=j: E.tensor_reduce(
                out=lnsum[:, j:j + 1], in_=uj, axis=AX.X, op=ALU.add),
                deps=[t_ld], sig=True)
            ts2 = S.op("act", lambda E, uj=uj, j=j: E.activation(
                out=lnsq[:, 0:D], in_=uj, func=AF.Square,
                accum_out=lnv[:, j:j + 1]), deps=[t_ld], sig=True)
            ts3 = S.op("dve", lambda E, j=j: E.tensor_scalar(
                out=lnm[:, j:j + 1], in0=lnsum[:, j:j + 1], scalar1=1.0 / D,
                scalar2=None, op0=ALU.mult), deps=[ts1], sig=True)
            ts4 = S.op("dve", lambda E, j=j: E.tensor_scalar(
                out=lnv[:, j:j + 1], in0=lnv[:, j:j + 1], scalar1=1.0 / D,
                scalar2=None, op0=ALU.mult), deps=[ts2, ts3], sig=True)
            ts5 = S.op("dve", lambda E, j=j: E.scalar_tensor_tensor(
                out=lnv[:, j:j + 1], in0=lnm[:, j:j + 1],
                scalar=lnm[:, j:j + 1], in1=lnv[:, j:j + 1], op0=ALU.mult,
                op1=ALU.subtract), deps=[ts4], sig=True)
            ts6 = S.op("dve", lambda E, j=j: E.tensor_scalar(
                out=lnv[:, j:j + 1], in0=lnv[:, j:j + 1], scalar1=-1.0,
                scalar2=1e-5, op0=ALU.mult, op1=ALU.add), deps=[ts5], sig=True)
            ts7 = S.op("act", lambda E, j=j: E.activation(
                out=lnr[:, j:j + 1], in_=lnv[:, j:j + 1], func=AF.Sqrt),
                deps=[ts6], sig=True)
            ts8 = S.op("dve", lambda E, j=j: E.reciprocal(
                out=lnr[:, j:j + 1], in_=lnr[:, j:j + 1]), deps=[ts7], sig=True)
            t9 = S.op("dve", lambda E, uj=uj, j=j: E.tensor_scalar(
                out=xn_tm[:, j * D:(j + 1) * D], in0=uj,
                scalar1=lnm[:, j:j + 1], scalar2=lnr[:, j:j + 1],
                op0=ALU.subtract, op1=ALU.mult),
                deps=[ts8, st_m["xn_free"]], sig=True)
            t_ln.append(t9)
        C[(m, warm, "ln")] = t_ln

    def stage_TXN(m, warm):
        tpe, evs = emit_transpose_evac(
            xn_tm, sc_d, xnf_h, xnf_l, dc_d, KC * TS, 0, 1,
            deps=C[(m, warm, "ln")], evac_deps=[st_m["xnf_free"]])
        C[(m, warm, "txn_pe")] = tpe
        C[(m, warm, "txn_ev")] = evs

    def stage_MMB(m, warm):
        evs_xn = C[(m, warm, "txn_ev")]
        t_mmb = []
        for half in range(2):
            sl, th, tl = stream_chunk("B", half * 4096, 4096)
            for o4 in range(4):
                oc = half * 4 + o4
                bank = oc // 2
                part = oc % 2
                ps = pb[bank][:, part * MT:(part + 1) * MT]
                pairs = []
                for k in range(KC):
                    pairs.append((
                        ring_h[sl][:, k * 512 + o4 * P:k * 512 + (o4 + 1) * P],
                        ring_l[sl][:, k * 512 + o4 * P:k * 512 + (o4 + 1) * P],
                        xnf_h[:, k * MT:(k + 1) * MT],
                        xnf_l[:, k * MT:(k + 1) * MT]))
                d = [th, tl, evs_xn[-1]] + bank_dep(bank)
                emit_mm3(ps, pairs, True, False, False, d)
                S.op("pe", lambda E, ps=ps, oc=oc: E.matmul(
                    ps, bBh[:, oc * P:(oc + 1) * P], ones_bf[:, 0:MT],
                    start=False, stop=False), deps=[t_bias, t_ones])
                tok = S.op("pe", lambda E, ps=ps, oc=oc: E.matmul(
                    ps, bBl[:, oc * P:(oc + 1) * P], ones_bf[:, 0:MT],
                    start=False, stop=True), sig=True)
                t_mmb.append(tok)
            slot_consumed(sl, t_mmb[-1])
        C[(m, warm, "mmb")] = t_mmb

    def stage_SCAN(m, warm):
        t_mmb = C[(m, warm, "mmb")]
        t_scan = []
        for k in range(KC):
            bank = k // 2
            part = k % 2
            d0 = abar[:, k:k + 1].broadcast_to([P, MT])
            t_sc = S.op("dve", lambda E, k=k, bank=bank, part=part, d0=d0:
                        E.tensor_tensor_scan(
                out=xsf[:, k * MT:(k + 1) * MT], data0=d0,
                data1=pb[bank][:, part * MT:(part + 1) * MT],
                initial=state[:, k:k + 1], op0=ALU.mult, op1=ALU.add),
                deps=[t_mmb[2 * bank + 1], t_abar, st_m["state_ok"],
                      st_m["scan_prev"], st_m["xs_free"]], sig=True)
            t_scan.append(t_sc)
        t_cp = None
        for k in range(KC):
            t_cp = S.op("dve", lambda E, k=k: E.tensor_copy(
                out=state[:, k:k + 1],
                in_=xsf[:, (k + 1) * MT - 1:(k + 1) * MT]),
                deps=[t_scan[k]], sig=True)
        st_m["state_ok"] = t_cp
        st_m["scan_prev"] = t_cp
        for k in range(KC):
            pb_free[k // 2].append(t_scan[k])

    def stage_XSPLIT(m):
        txh = S.op("act", lambda E: E.activation(out=xsf_h[:, :], in_=xsf[:, :],
                                                 func=AF.Copy),
                   deps=[st_m["scan_prev"], st_m["xsfhl_free"]], sig=True)
        txl = S.op("dve", lambda E: E.tensor_tensor(
            out=xsf_l[:, :], in0=xsf[:, :], in1=xsf_h[:, :], op=ALU.subtract),
            deps=[txh], sig=True)
        st_m["xs_free"] = txl
        C[(m, False, "xsplit")] = txl

    def stage_QX(m):
        t_q = []
        for j in range(TS):
            t_q.append(emit_quant(xn_tm[:, j * D:(j + 1) * D],
                                  xn_tm[:, j * D:(j + 1) * D], D,
                                  [C[(m, False, "ln")][j],
                                   C[(m, False, "txn_pe")]]))
        C[(m, False, "qx")] = t_q

    def stage_TXQ(m):
        tpe, evs = emit_transpose_evac(
            xn_tm, sc_d, xqf_h, xqf_l, dc_d, KC * TS, 0, 1,
            deps=C[(m, False, "qx")], evac_deps=[st_m["xqf_free"]])
        st_m["xn_free"] = tpe
        C[(m, False, "txq_ev")] = evs

    GU_PARTS = ((0, 1), (2, 3), (4, 5))
    QH_REG = ((0, 1024), (1024, 1024), (2048, 768))

    def stage_GU(m, part):
        evs_xq = C[(m, False, "txq_ev")]
        if part == 0:
            C[(m, False, "mul")] = {}
        t_mul = C[(m, False, "mul")]
        last = None
        for oc in GU_PARTS[part]:
            w = 512 if oc < 5 else 256
            slg, thg, tlg = stream_chunk("G", oc * 4096, KC * w)
            slu, thu, tlu = stream_chunk("U", oc * 4096, KC * w)
            for j in range(TS):
                bkg, bku = 4 + j, 6 + j
                pairsg, pairsu = [], []
                for k in range(KC):
                    lh = xqf_h[:, k * MT + j * P:k * MT + (j + 1) * P]
                    ll = xqf_l[:, k * MT + j * P:k * MT + (j + 1) * P]
                    pairsg.append((lh, ll, ring_h[slg][:, k * w:(k + 1) * w],
                                   ring_l[slg][:, k * w:(k + 1) * w]))
                    pairsu.append((lh, ll, ring_h[slu][:, k * w:(k + 1) * w],
                                   ring_l[slu][:, k * w:(k + 1) * w]))
                tg = emit_mm3(pb[bkg][:, 0:w], pairsg, True, True, True,
                              [thg, tlg, evs_xq[-1]] + bank_dep(bkg))
                tu = emit_mm3(pb[bku][:, 0:w], pairsu, True, True, True,
                              [thu, tlu] + bank_dep(bku))
                last = tu
                hcol = j * I + oc * 512
                tsil = S.op("act", lambda E, b=bkg, hcol=hcol, w=w: E.activation(
                    out=h_tm[:, hcol:hcol + w], in_=pb[b][:, 0:w], func=AF.Silu),
                    deps=[tg, st_m["h_free"]], sig=True)
                tm = S.op("dve", lambda E, b=bku, hcol=hcol, w=w: E.tensor_tensor(
                    out=h_tm[:, hcol:hcol + w], in0=h_tm[:, hcol:hcol + w],
                    in1=pb[b][:, 0:w], op=ALU.mult), deps=[tsil, tu], sig=True)
                t_mul[(j, oc)] = tm
                pb_free[bkg].append(tsil)
                pb_free[bku].append(tm)
            slot_consumed(slg, t_mul[(TS - 1, oc)])
            slot_consumed(slu, t_mul[(TS - 1, oc)])
        if part == len(GU_PARTS) - 1:
            st_m["xqf_free"] = last

    def stage_QHr(m, part):
        t_mul = C[(m, False, "mul")]
        c0r, wr = QH_REG[part]
        t_qh = C.setdefault((m, False, "qh"), [])
        for j in range(TS):
            c0 = j * I + c0r
            deps = [t_mul[(j, oc)] for oc in GU_PARTS[part]]
            t_qh.append(emit_quant(h_tm[:, c0:c0 + wr], h_tm[:, c0:c0 + wr],
                                   wr, deps))

    def stage_CD(m):
        txl = C[(m, False, "xsplit")]
        t_cd = {}
        for oc in range(2):
            slc, thc, tlc = stream_chunk("C", oc * 4096, 4096)
            sld, thd, tld = stream_chunk("D", oc * 4096, 4096)
            for j in range(TS):
                bank = 4 + j * 2 + oc
                ps = pb[bank][:, 0:512]
                pairs = []
                for k in range(KC):
                    pairs.append((
                        xsf_h[:, k * MT + j * P:k * MT + (j + 1) * P],
                        xsf_l[:, k * MT + j * P:k * MT + (j + 1) * P],
                        ring_h[slc][:, k * 512:(k + 1) * 512],
                        ring_l[slc][:, k * 512:(k + 1) * 512]))
                emit_mm3(ps, pairs, True, False, False,
                         [thc, tlc, txl] + bank_dep(bank))
                pairs = []
                for k in range(KC):
                    pairs.append((
                        xnf_h[:, k * MT + j * P:k * MT + (j + 1) * P],
                        xnf_l[:, k * MT + j * P:k * MT + (j + 1) * P],
                        ring_h[sld][:, k * 512:(k + 1) * 512],
                        ring_l[sld][:, k * 512:(k + 1) * 512]))
                emit_mm3(ps, pairs, False, False, False, [thd, tld])
                for bi, (bv, stp) in enumerate(((bCh, False), (bCl, False),
                                                (bDh, False), (bDl, True))):
                    tok = S.op("pe", lambda E, ps=ps, oc=oc, bv=bv, stp=stp:
                               E.matmul(
                        ps, ones_bf[:, 0:P], bv[:, oc * 512:(oc + 1) * 512],
                        start=False, stop=stp), deps=[t_bias] if bi == 0 else (),
                        sig=stp)
                t_cd[(j, oc)] = tok
            slot_consumed(slc, t_cd[(TS - 1, oc)])
            slot_consumed(sld, t_cd[(TS - 1, oc)])
        st_m["xnf_free"] = t_cd[(TS - 1, 1)]
        st_m["xsfhl_free"] = t_cd[(TS - 1, 1)]
        C[(m, False, "cd")] = t_cd

    def stage_THQ(m):
        tpe, evs = emit_transpose_evac(
            h_tm, sc_h, hqf_h, hqf_l, dc_h, IC * TS, 0, 1,
            deps=C[(m, False, "qh")], evac_deps=[st_m["hqf_free"]])
        st_m["h_free"] = tpe
        C[(m, False, "thq_ev")] = evs

    def stage_DWN(m):
        evs_hq = C[(m, False, "thq_ev")]
        t_down = {}
        tok = None
        for oc in range(2):
            for pi, (k0, nk) in enumerate(DN_PIECES):
                sl, th, tl = stream_chunk("N", (oc * 22 + k0) * 512, nk * 512)
                for j in range(TS):
                    bank = 2 + j
                    ps = pb[bank][:, 0:512]
                    pairs = []
                    for kk in range(nk):
                        kf = k0 + kk
                        pairs.append((
                            hqf_h[:, kf * MT + j * P:kf * MT + (j + 1) * P],
                            hqf_l[:, kf * MT + j * P:kf * MT + (j + 1) * P],
                            ring_h[sl][:, kk * 512:(kk + 1) * 512],
                            ring_l[sl][:, kk * 512:(kk + 1) * 512]))
                    d = [th, tl, evs_hq[-1]]
                    if pi == 0:
                        d += bank_dep(bank)
                    tok = emit_mm3(ps, pairs, pi == 0,
                                   pi == len(DN_PIECES) - 1, True, d)
                slot_consumed(sl, tok)
            for j in range(TS):
                bank = 2 + j
                gcol = j * D + oc * 512
                tg2 = S.op("act", lambda E, b=bank, gcol=gcol: E.activation(
                    out=g_tm[:, gcol:gcol + 512], in_=pb[b][:, 0:512],
                    func=AF.Silu), deps=[tok, st_m["g_free"]], sig=True)
                t_down[(j, oc)] = tg2
                pb_free[bank].append(tg2)
        st_m["hqf_free"] = tok
        C[(m, False, "dwn")] = t_down

    def stage_OUT(m):
        buf = m % 2
        t_cd = C[(m, False, "cd")]
        t_down = C[(m, False, "dwn")]
        t_comb = None
        for j in range(TS):
            for oc in range(2):
                bank = 4 + j * 2 + oc
                gcol = j * D + oc * 512
                t1 = S.op("dve", lambda E, b=bank, gcol=gcol: E.tensor_tensor(
                    out=g_tm[:, gcol:gcol + 512], in0=g_tm[:, gcol:gcol + 512],
                    in1=pb[b][:, 0:512], op=ALU.mult),
                    deps=[t_down[(j, oc)], t_cd[(j, oc)]], sig=True)
                t_comb = S.op("dve", lambda E, gcol=gcol, buf=buf: E.tensor_tensor(
                    out=u_tm[buf][:, gcol:gcol + 512],
                    in0=u_tm[buf][:, gcol:gcol + 512],
                    in1=g_tm[:, gcol:gcol + 512], op=ALU.add),
                    deps=[t1], sig=True)
                pb_free[bank].append(t_comb)
        st_m["g_free"] = t_comb
        orow = m * MT
        o_v = out_d[orow:orow + MT, :].rearrange("(j p) f -> p j f", p=P)
        i_v = u_tm[buf][:, 0:TS * D].rearrange("p (j f) -> p j f", f=D)
        t_o = S.op("gp", lambda E: E.dma_start(out=o_v, in_=i_v),
                   deps=[t_comb], sig=True)
        st_m["u"][buf] = t_o

    def front(m, warm):
        stage_LD(m, warm)
        stage_LN(m, warm)
        stage_TXN(m, warm)
        stage_MMB(m, warm)
        stage_SCAN(m, warm)
        if warm:
            st_m["u"][m % 2] = C[(m, warm, "ln")][-1]
            st_m["xnf_free"] = C[(m, warm, "mmb")][-1]
        else:
            stage_XSPLIT(m)
            stage_QX(m)
            stage_TXQ(m)

    for wm in range(NWARM):
        front(wm, True)
    st_m["state_ok"] = S.op("dve", lambda E: E.tensor_scalar(
        out=state[:, :], in0=state[:, :], scalar1=wsc[:, 0:1], scalar2=None,
        op0=ALU.mult), deps=[st_m["state_ok"], t_const], sig=True)
    st_m["scan_prev"] = st_m["state_ok"]

    front(0, False)
    for m in range(NMAC):
        stage_GU(m, 0)
        stage_QHr(m, 0)
        if m + 1 < NMAC:
            stage_LD(m + 1, False)
            stage_LN(m + 1, False)
        stage_GU(m, 1)
        stage_QHr(m, 1)
        stage_GU(m, 2)
        stage_QHr(m, 2)
        stage_CD(m)
        if m + 1 < NMAC:
            stage_TXN(m + 1, False)
        stage_THQ(m)
        if m + 1 < NMAC:
            stage_QX(m + 1)
        stage_DWN(m)
        stage_OUT(m)
        if m + 1 < NMAC:
            stage_MMB(m + 1, False)
            stage_SCAN(m + 1, False)
            stage_XSPLIT(m + 1)
            stage_TXQ(m + 1)

    S.op("gp", lambda E: E.wait_ge(S.sem["gp"], S.cnt["gp"]))
    S.op("sp", lambda E: E.wait_ge(S.sem["sp"], S.cnt["sp"]))

    with nc.Block() as block:
        @block.tensor
        def _(E):
            for fn in S.q["pe"]:
                fn(E)

        @block.vector
        def _(E):
            for fn in S.q["dve"]:
                fn(E)

        @block.scalar
        def _(E):
            for fn in S.q["act"]:
                fn(E)

        @block.sync
        def _(E):
            for fn in S.q["sp"]:
                fn(E)

        @block.gpsimd
        def _(E):
            for fn in S.q["gp"]:
                fn(E)

    es.close()
    return nc


def kernel(**inputs):
    u = np.asarray(inputs["L0_attn_u"], dtype=np.float32)
    log_A = np.asarray(inputs["L0_attn_log_A"], dtype=np.float32)
    W_B = np.asarray(inputs["L0_attn_W_B"], dtype=np.float32)
    b_B = np.asarray(inputs["L0_attn_b_B"], dtype=np.float32)
    W_C = np.asarray(inputs["L0_attn_W_C"], dtype=np.float32)
    b_C = np.asarray(inputs["L0_attn_b_C"], dtype=np.float32)
    W_D = np.asarray(inputs["L0_attn_W_D"], dtype=np.float32)
    b_D = np.asarray(inputs["L0_attn_b_D"], dtype=np.float32)
    log_dt = np.asarray(inputs["L0_attn_log_dt"], dtype=np.float32)
    w_gate = np.asarray(inputs["L0_ffn_w_gate"], dtype=np.float32)
    w_up = np.asarray(inputs["L0_ffn_w_up"], dtype=np.float32)
    w_down = np.asarray(inputs["L0_ffn_w_down"], dtype=np.float32)

    WBt = np.ascontiguousarray(W_B.T)
    WCt = np.ascontiguousarray(W_C.T)
    WDt = np.ascontiguousarray(W_D.T)
    logA_t = np.ascontiguousarray(log_A.reshape(KC, P).T)
    ident = np.eye(P, dtype=np.float32)

    nc = build_program()
    in_maps = []
    for c in range(8):
        b, q = c // 4, c % 4
        t0 = q * T_CORE
        wstart = t0 - T_WARM if q > 0 else 0
        u_c = np.ascontiguousarray(
            np.concatenate([u[b, wstart:wstart + T_WARM], u[b, t0:t0 + T_CORE]],
                           axis=0))
        wsc_v = np.full((P, 1), 1.0 if q > 0 else 0.0, dtype=np.float32)
        in_maps.append({
            "u": u_c,
            "wbt": WBt, "wct": WCt, "wdt": WDt,
            "wg": w_gate, "wu": w_up, "wdn": w_down,
            "logA": logA_t, "ldt": log_dt.reshape(1, 1),
            "wsc": wsc_v,
            "bB": b_B.reshape(1, D), "bC": b_C.reshape(1, D),
            "bD": b_D.reshape(1, D), "ident": ident,
        })
    res = run_bass_kernel_spmd(nc, in_maps, core_ids=list(range(8)))
    out = np.empty((2, 16384, D), dtype=np.float32)
    for c in range(8):
        b, q = c // 4, c % 4
        out[b, q * T_CORE:(q + 1) * T_CORE] = res.results[c]["out"]
    return out
